# version 20
# baseline (speedup 1.0000x reference)
import math
import numpy as np

import concourse.bass as bass
import concourse.mybir as mybir
from concourse.bass_utils import run_bass_kernel_spmd

F32 = mybir.dt.float32
BF16 = mybir.dt.bfloat16
AF = mybir.ActivationFunctionType
ALU = mybir.AluOpType
AX = mybir.AxisListType

D = 4096
DH = 128
H = 32
TOWN = 1024
TEXT = 1536
NT = 8
NTX = 12
OC = 256
NSLAB = D // OC
EPS = 1e-5
SCALE = 1.0 / float(np.sqrt(DH))
NCORES = 8


def _rope_cache(S=4096, Dh=DH, base=10000.0):
    inv_freq = 1.0 / base ** (np.arange(0, Dh, 2, dtype=np.float32) / Dh)
    t = np.arange(S, dtype=np.float32)
    f = np.outer(t, inv_freq)
    cos = np.zeros((S, Dh), dtype=np.float32)
    sin = np.zeros((S, Dh), dtype=np.float32)
    cos[:, 0::2] = np.cos(f); cos[:, 1::2] = np.cos(f)
    sin[:, 0::2] = np.sin(f); sin[:, 1::2] = np.sin(f)
    return cos, sin


class Sem:
    __slots__ = ("h", "n", "name")
    def __init__(self, h, name):
        self.h = h; self.n = 0; self.name = name


ENGINES = ("sync", "gpsimd", "tensor", "vector", "scalar")


class Prog:
    def __init__(self, nc):
        self.nc = nc
        self.ops = {e: [] for e in ENGINES}

    def sem(self, name):
        return Sem(self.nc.alloc_semaphore(name=name), name)

    def op(self, eng, fn, waits=(), inc=None, inc_by=1):
        if inc is not None:
            inc.n += inc_by
        self.ops[eng].append((fn, tuple(waits), inc, inc_by))

    def emit(self):
        block = self.nc.Block()
        with block:
            for eng_name in ENGINES:
                oplist = self.ops[eng_name]
                def make(oplist):
                    def body(eng):
                        for fn, waits, inc, inc_by in oplist:
                            for s, v in waits:
                                if v > 0:
                                    eng.wait_ge(s.h, v)
                            inst = fn(eng)
                            if inc is not None:
                                if isinstance(inst, list):
                                    inst = inst[-1]
                                inst.then_inc(inc.h, inc_by)
                    return body
                getattr(block, eng_name)(make(oplist))


class Ring:
    def __init__(self, payloads, init_conds=None):
        self.slots = [[p, list(init_conds or [])] for p in payloads]
        self.i = 0
    def acquire(self):
        s = self.slots[self.i % len(self.slots)]
        self.i += 1
        return s[0], list(s[1])
    def release_last(self, *conds):
        s = self.slots[(self.i - 1) % len(self.slots)]
        s[1] = list(conds)


def build_program():
    nc = bass.Bass("TRN2", target_bir_lowering=False)

    def din(name, shape, dt=F32):
        return nc.dram_tensor(name, list(shape), dt, kind="ExternalInput").ap()
    x_ext = din("x_ext", (TEXT, D))
    wqT = din("wqT", (D, D))
    wkT = din("wkT", (D, DH))
    wvT = din("wvT", (D, DH))
    woT = din("woT", (D, D))
    baseT = din("baseT", (D, D))
    routerT = din("routerT", (D, 8))
    a_all = din("a_all", (D, 128))
    b_all = din("b_all", (128, D))
    cos2 = din("cos2", (TOWN, 2 * DH))
    sinp2 = din("sinp2", (TOWN, 2 * DH))
    cosk = din("cosk", (TEXT, DH))
    sinkp = din("sinkp", (TEXT, DH))
    kbias = din("kbias", (128, 24))
    alpha_exp = din("alpha_exp", (128, 64))
    identity = din("identity", (128, 128))
    onescol = din("onescol", (128, 1))
    ones16 = din("ones16", (128, 16))
    out_hbm = nc.dram_tensor("out", [TOWN, D], F32, kind="ExternalOutput").ap()
    x1_dram = nc.dram_tensor("x1_scratch", [TOWN, D], BF16, kind="Internal").ap()

    BASE = 16384 + 64
    CONST = BASE
    SLAB0 = BASE + 2560
    SLAB1 = SLAB0 + 16384
    BIG1 = SLAB1 + 16384
    BIG2 = BIG1 + 65536
    FREG = BIG2 + 65536

    def sb(name, shape, dt, off):
        return nc.alloc_sbuf_tensor_at(name, list(shape), dt, offset=off)

    identB  = sb("identB", (128, 128), BF16, CONST + 0)
    onesc   = sb("onesc", (128, 1), BF16, CONST + 256)
    kbias_s = sb("kbias_s", (128, 24), F32, CONST + 288)
    alpha_s = sb("alpha_s", (128, 64), F32, CONST + 384)
    ones16_s= sb("ones16_s", (128, 16), F32, CONST + 640)
    bn6 = [sb(f"bn6_{i}", (128, 8, 6), F32, CONST + 704 + 192 * i) for i in range(2)]
    mv  = [sb(f"mv_{i}", (128, 2), F32, CONST + 1088 + 32 * i) for i in range(2)]
    lnv = [sb(f"lnv_{i}", (128, 1), F32, CONST + 1152 + 32 * i) for i in range(2)]
    rstd= [sb(f"rstd_{i}", (128, 1), F32, CONST + 1216 + 32 * i) for i in range(2)]
    negmr=[sb(f"negmr_{i}", (128, 1), F32, CONST + 1280 + 32 * i) for i in range(2)]
    ur16 = sb("ur16", (128, 16), F32, CONST + 1344)
    u16  = sb("u16", (128, 16), F32, CONST + 1792)
    m1r  = sb("m1r", (128, 1), F32, CONST + 1440)
    m2r  = sb("m2r", (128, 1), F32, CONST + 1472)
    gsr  = sb("gsr", (128, 1), F32, CONST + 1504)
    rgsr = sb("rgsr", (128, 1), F32, CONST + 1536)
    dlog = sb("dlog", (128, 8), F32, CONST + 1568)
    eqm  = sb("eqm", (128, 8), F32, CONST + 1600)
    dmask= sb("dmask", (128, 8), F32, CONST + 1632)
    edl  = sb("edl", (128, 8), F32, CONST + 1664)
    gate = sb("gate", (128, 8), F32, CONST + 1696)
    w8   = sb("w8", (128, 8), F32, CONST + 1728)
    t4r  = sb("t4r", (128, 4), F32, CONST + 1760)

    slabs = [sb(f"slab{i}", (128, 32, OC), BF16, SLAB0 if i == 0 else SLAB1)
             for i in range(2)]

    ln1T_own = sb("ln1T_own", (128, 32, TOWN), BF16, BIG1)
    attn_T   = sb("attn_T", (128, 32, TOWN), BF16, BIG1)
    ln2T     = sb("ln2T", (128, 32, TOWN), BF16, BIG1)

    xstage = [sb(f"xstage{i}", (128, D), F32, BIG2 + 16384 * i) for i in range(2)]
    wkT_s  = sb("wkT_s", (128, 32, DH), BF16, BIG2 + 32768)
    wvT_s  = sb("wvT_s", (128, 32, DH), BF16, BIG2 + 40960)
    cosk_s = sb("cosk_s", (128, NTX, DH), F32, BIG2 + 49152)
    sinkp_s= sb("sinkp_s", (128, NTX, DH), F32, BIG2 + 55296)
    qrT    = sb("qrT", (128, 32, TOWN), BF16, BIG2)
    proj   = sb("proj", (128, NT, D), BF16, BIG2)
    y_nat  = sb("y_nat", (128, NT, D), BF16, BIG2)

    ln1T_halo = sb("ln1T_halo", (128, 32, 512), BF16, FREG)
    xt_stage  = sb("xt_stage", (128, D), BF16, FREG + 32768)
    v_sb      = sb("v_sb", (128, NTX, DH), BF16, FREG + 40320)
    krT       = sb("krT", (128, TEXT), BF16, FREG + 43392)
    cos2_s  = sb("cos2_s", (128, NT, 2 * DH), F32, FREG)
    sinp2_s = sb("sinp2_s", (128, NT, 2 * DH), F32, FREG + 8192)
    qm1 = [sb(f"qm1_{i}", (128, 2 * DH), BF16, FREG + 16384 + 512 * i) for i in range(2)]
    qm2 = [sb(f"qm2_{i}", (128, 2 * DH), BF16, FREG + 17408 + 512 * i) for i in range(2)]
    km1 = [sb(f"km1_{i}", (128, DH), BF16, FREG + 32768 + 256 * i) for i in range(2)]
    km2 = [sb(f"km2_{i}", (128, DH), BF16, FREG + 33280 + 256 * i) for i in range(2)]
    sig = [sb(f"sig_{i}", (128, 6, 4, 256), BF16, FREG + 12288 * i) for i in range(2)]
    bl1 = [sb(f"bl1_{i}", (128, 128), F32, FREG + 24576 + 512 * i) for i in range(4)]
    bl2 = [sb(f"bl2_{i}", (128, 128), BF16, FREG + 28672 + 256 * i) for i in range(8)]
    xre    = sb("xre", (128, D), BF16, FREG)
    x1b    = [sb(f"x1b_{i}", (128, D), BF16, FREG + 16384 + 8192 * i) for i in range(2)]
    outstage = [sb(f"outstage_{i}", (128, D), F32, FREG + 24576 * i) for i in range(2)]
    x1re     = sb("x1re", (128, D), BF16, FREG + 16384)
    a_all_s  = sb("a_all_s", (128, 32, 128), BF16, FREG + 32768)
    routerT_s= sb("routerT_s", (128, 32, 8), BF16, FREG + 40960)
    midgT    = sb("midgT", (128, NT, 128), BF16, FREG + 41472)
    midg  = [sb(f"midg_{i}", (128, 128), BF16, FREG + 45568 + 256 * i) for i in range(2)]
    router_sb = [sb(f"router_sb{i}", (128, 136), F32, FREG + 8192 + 544 * i + (32 - 544 % 32) * i) for i in range(2)]
    brider = [sb(f"brider_{i}", (128, OC), BF16, FREG + 43520 + 512 * i) for i in range(2)]
    wex      = sb("wex", (128, 128), F32, FREG + 44544)

    import contextlib
    es = contextlib.ExitStack()
    ps_all = es.enter_context(nc.psum_tensor("ps_all", [128, 4096], F32))
    ps = [ps_all[:, 512 * i:512 * (i + 1)] for i in range(8)]

    P = Prog(nc)
    s_xin   = [P.sem("s_xin0"), P.sem("s_xin1")]
    s_const = P.sem("s_const")
    s_wkv   = P.sem("s_wkv")
    s_slab  = [P.sem("s_slab0"), P.sem("s_slab1")]
    s_chain = P.sem("s_chain")
    s_rstd  = P.sem("s_rstd")
    s_negmr = P.sem("s_negmr")
    s_apply = P.sem("s_apply")
    s_tp    = P.sem("s_tp")
    s_tpe   = P.sem("s_tpe")
    s_tpe2  = P.sem("s_tpe2")
    s_kvmm  = P.sem("s_kvmm")
    s_rope  = P.sem("s_rope")
    s_vev   = P.sem("s_vev")
    s_qmm   = P.sem("s_qmm")
    s_qtp   = P.sem("s_qtp")
    s_qev   = P.sem("s_qev")
    s_sc    = P.sem("s_sc")
    s_sig   = P.sem("s_sig")
    s_av    = P.sem("s_av")
    s_u     = P.sem("s_u")
    s_bl    = P.sem("s_bl")
    s_atp   = P.sem("s_atp")
    s_ate   = P.sem("s_ate")
    s_pmm   = P.sem("s_pmm")
    s_pev   = P.sem("s_pev")
    s_xre   = P.sem("s_xre")
    s_x1w   = [P.sem("s_x1w0"), P.sem("s_x1w1")]
    s_x1r   = P.sem("s_x1r")
    s_mmm   = P.sem("s_mmm")
    s_mgT   = P.sem("s_mgT")
    s_ymm   = P.sem("s_ymm")
    s_yev   = P.sem("s_yev")
    s_out   = [P.sem("s_out0"), P.sem("s_out1")]
    s_fw    = P.sem("s_fw")
    s_gc    = P.sem("s_gc")

    P.op("sync", lambda e: e.dma_start(out=kbias_s[:, :], in_=kbias[:, :]), inc=s_const, inc_by=16)
    P.op("sync", lambda e: e.dma_start(out=alpha_s[:, :], in_=alpha_exp[:, :]), inc=s_const, inc_by=16)
    P.op("sync", lambda e: e.dma_start(out=ones16_s[:, :], in_=ones16[:, :]), inc=s_const, inc_by=16)
    P.op("sync", lambda e: e.dma_start(out=cosk_s[:, :, :],
                                       in_=cosk.rearrange("(t p) c -> p t c", p=128)), inc=s_const, inc_by=16)
    P.op("sync", lambda e: e.dma_start(out=sinkp_s[:, :, :],
                                       in_=sinkp.rearrange("(t p) c -> p t c", p=128)), inc=s_const, inc_by=16)
    P.op("gpsimd", lambda e: e.dma_start(out=identB[:, :], in_=identity[:, :]), inc=s_gc, inc_by=16)
    P.op("gpsimd", lambda e: e.dma_start(out=onesc[:, :], in_=onescol[:, :]), inc=s_gc, inc_by=16)
    P.op("gpsimd", lambda e: e.dma_start(out=wkT_s[:, :, :],
                                         in_=wkT.rearrange("(t p) c -> p t c", p=128)), inc=s_wkv, inc_by=16)
    P.op("gpsimd", lambda e: e.dma_start(out=wvT_s[:, :, :],
                                         in_=wvT.rearrange("(t p) c -> p t c", p=128)), inc=s_wkv, inc_by=16)
    CONST_N = s_const.n
    GC_N = s_gc.n
    WKV_N = s_wkv.n

    ln_par = [0]
    ln_free = {0: 0, 1: 0}
    def emit_ln_stats_apply(src_ap, dst_ap, waits, extra_apply_waits=()):
        par = ln_par[0] & 1
        ln_par[0] += 1
        b6, m, lv, rs, nm = bn6[par], mv[par], lnv[par], rstd[par], negmr[par]
        w0 = list(waits) + [(s_apply, ln_free[par])]
        for j in range(8):
            def f(e, j=j, b6=b6, src=src_ap):
                return e.bn_stats(b6[:, j, :], src[:, j * 512:(j + 1) * 512])
            P.op("vector", f, waits=(w0 if j == 0 else ()),
                 inc=(s_chain if j == 7 else None))
        c_bn = s_chain.n
        P.op("vector", lambda e, b6=b6, m=m: e.bn_aggr(m[:, :], b6[:, :, :]),
             waits=[(s_chain, c_bn)], inc=s_chain)
        c_mv = s_chain.n
        P.op("vector", lambda e, m=m, lv=lv: e.tensor_scalar(lv[:, :], m[:, 1:2], EPS, None, ALU.add),
             waits=[(s_chain, c_mv)], inc=s_chain)
        c_ve = s_chain.n
        P.op("scalar", lambda e, lv=lv, lv2=lv: e.activation(out=lv[:, :], in_=lv[:, :], func=AF.Ln),
             waits=[(s_chain, c_ve)], inc=s_rstd)
        c_ln = s_rstd.n
        P.op("scalar", lambda e, lv=lv, rs=rs: e.activation(out=rs[:, :], in_=lv[:, :], func=AF.Exp, scale=-0.5),
             waits=[(s_rstd, c_ln)], inc=s_rstd)
        c_rs = s_rstd.n
        P.op("vector", lambda e, m=m, rs=rs, nm=nm: e.tensor_scalar(nm[:, :], m[:, 0:1], rs[:, :], -1.0, ALU.mult, ALU.mult),
             waits=[(s_rstd, c_rs)], inc=s_negmr)
        c_nm = s_negmr.n
        P.op("scalar", lambda e, src=src_ap, dst=dst_ap, rs=rs, nm=nm: e.activation(
                out=dst, in_=src, func=AF.Identity, scale=rs[:, :], bias=nm[:, :]),
             waits=[(s_negmr, c_nm)] + list(extra_apply_waits), inc=s_apply)
        ln_free[par] = s_apply.n
        return s_apply.n

    tpb_ring = Ring([ps[0], ps[1]])
    def emit_transpose_tile(src_tile_ap, dst_view_fn, src_waits, evac_eng="scalar"):
        for g in range(8):
            bank, bwaits = tpb_ring.acquire()
            for k in range(4):
                dcol = 4 * g + k
                def f(e, bank=bank, k=k, dcol=dcol, src=src_tile_ap):
                    return e.matmul(bank[:, k * 128:(k + 1) * 128],
                                    src[:, dcol * 128:(dcol + 1) * 128],
                                    identB[:, :], start=True, stop=True)
                P.op("tensor", f,
                     waits=(list(src_waits) + bwaits + [(s_gc, GC_N)]) if k == 0 else (),
                     inc=(s_tp if k == 3 else None))
            c_tp = s_tp.n
            eng = ("scalar" if (g & 1) == 0 else "vector") if evac_eng == "alt" else evac_eng
            if eng == "scalar":
                def fe(e, bank=bank, g=g):
                    return e.activation(out=dst_view_fn(g), in_=bank[:, :], func=AF.Copy)
            else:
                def fe(e, bank=bank, g=g):
                    return e.tensor_copy(dst_view_fn(g), bank[:, :])
            sem = s_tpe if eng == "scalar" else s_tpe2
            P.op(eng, fe, waits=[(s_tp, c_tp)], inc=sem)
            tpb_ring.release_last((sem, sem.n))
        return [(s_tpe, s_tpe.n), (s_tpe2, s_tpe2.n)]

    xs_ring = Ring([0, 1])
    a_evac_counts = [0] * NTX

    def ln1T_view(dt, ti):
        if 2 <= ti < 10:
            return ln1T_own[:, dt, (ti - 2) * 128:(ti - 1) * 128]
        hi = {0: 0, 1: 1, 10: 2, 11: 3}[ti]
        return ln1T_halo[:, dt, hi * 128:(hi + 1) * 128]

    def emit_a(ti):
        par, xw = xs_ring.acquire()
        xb = xstage[par]
        def fl(e, ti=ti, xb=xb):
            return e.dma_start(out=xb[:, :], in_=x_ext[ti * 128:(ti + 1) * 128, :])
        P.op("sync", fl, waits=xw, inc=s_xin[par], inc_by=16)
        c_x = s_xin[par].n
        c_ap = emit_ln_stats_apply(xb[:, :], xt_stage[:, :],
                                   waits=[(s_xin[par], c_x)],
                                   extra_apply_waits=[(s_tpe, s_tpe.n), (s_tpe2, s_tpe2.n)])
        xs_ring.release_last((s_apply, c_ap))
        if 2 <= ti < 10:
            tloc = ti - 2
            def dvf(g, tloc=tloc):
                return ln1T_own[:, 4 * g:4 * g + 4, tloc * 128:(tloc + 1) * 128]
        else:
            hi = {0: 0, 1: 1, 10: 2, 11: 3}[ti]
            def dvf(g, hi=hi):
                return ln1T_halo[:, 4 * g:4 * g + 4, hi * 128:(hi + 1) * 128]
        a_evac_counts[ti] = emit_transpose_tile(xt_stage[:, :], dvf,
                                                src_waits=[(s_apply, c_ap)])

    kv_ring = Ring([(ps[2], ps[3]), (ps[4], ps[5])])
    ktp_ring = Ring([ps[6], ps[7]])

    def emit_b0(ti):
        (kb, vb), kvw = kv_ring.acquire()
        for d in range(32):
            def f(e, d=d, ti=ti, kb=kb):
                return e.matmul(kb[:, 0:128], ln1T_view(d, ti), wkT_s[:, d, :],
                                start=(d == 0), stop=(d == 31))
            P.op("tensor", f,
                 waits=(kvw + [(s_wkv, WKV_N)] + a_evac_counts[ti]) if d == 0 else ())
            def f2(e, d=d, ti=ti, vb=vb):
                return e.matmul(vb[:, 0:128], ln1T_view(d, ti), wvT_s[:, d, :],
                                start=(d == 0), stop=(d == 31))
            P.op("tensor", f2, inc=(s_kvmm if d == 31 else None))
        c_kv = s_kvmm.n
        kp = ti & 1
        P.op("vector", lambda e, kb=kb, ti=ti, kp=kp: e.tensor_mul(
                km1[kp][:, :], kb[:, 0:128], cosk_s[:, ti, :]),
             waits=[(s_kvmm, c_kv), (s_const, CONST_N), (s_tp, s_tp.n)], inc=s_rope)
        P.op("vector", lambda e, kb=kb, ti=ti, kp=kp: e.tensor_mul(
                km2[kp][:, 0:64], kb[:, 64:128], sinkp_s[:, ti, 0:64]))
        P.op("vector", lambda e, kb=kb, ti=ti, kp=kp: e.tensor_mul(
                km2[kp][:, 64:128], kb[:, 0:64], sinkp_s[:, ti, 64:128]), inc=s_rope)
        c_rope = s_rope.n
        P.op("scalar", lambda e, vb=vb, ti=ti: e.activation(
                out=v_sb[:, ti, :], in_=vb[:, 0:128], func=AF.Copy),
             waits=[(s_kvmm, c_kv)], inc=s_vev)
        kv_ring.release_last((s_rope, c_rope), (s_vev, s_vev.n))
        tb, tw = ktp_ring.acquire()
        P.op("tensor", lambda e, tb=tb, kp=kp: e.matmul(
                tb[:, 0:128], km1[kp][:, :], identB[:, :], start=True, stop=False),
             waits=tw + [(s_rope, c_rope)])
        P.op("tensor", lambda e, tb=tb, kp=kp: e.matmul(
                tb[:, 0:128], km2[kp][:, :], identB[:, :], start=False, stop=True), inc=s_tp)
        c_ktp = s_tp.n
        P.op("scalar", lambda e, tb=tb, ti=ti: e.activation(
                out=krT[:, ti * 128:(ti + 1) * 128], in_=tb[:, 0:128], func=AF.Copy),
             waits=[(s_tp, c_ktp)], inc=s_tpe)
        ktp_ring.release_last((s_tpe, s_tpe.n))

    for ti in range(NTX):
        emit_a(ti)
    for ti in range(NTX):
        emit_b0(ti)
    B0_DONE = [(s_tpe, s_tpe.n), (s_tpe2, s_tpe2.n)]
    B0_KV = s_kvmm.n
    s_c2 = P.sem("s_c2")
    P.op("sync", lambda e: e.dma_start(out=cos2_s[:, :, :],
                                       in_=cos2.rearrange("(t p) c -> p t c", p=128)),
         waits=[(s_kvmm, B0_KV)], inc=s_c2, inc_by=16)
    P.op("sync", lambda e: e.dma_start(out=sinp2_s[:, :, :],
                                       in_=sinp2.rearrange("(t p) c -> p t c", p=128)),
         waits=[(s_kvmm, B0_KV)], inc=s_c2, inc_by=16)
    C2_N = s_c2.n

    slab_ring = Ring([0, 1])
    q_ring = Ring([ps[0], ps[1], ps[2], ps[3]],
                  init_conds=list(B0_DONE))
    qtp_ring = Ring([ps[4], ps[5]], init_conds=list(B0_DONE))
    qm_free = {0: [], 1: []}

    def emit_slab_load(src, s, rider=False):
        par, sw = slab_ring.acquire()
        def f(e, src=src, s=s, par=par):
            return e.dma_start(
                out=slabs[par][:, :, :],
                in_=src[:, s * OC:(s + 1) * OC].rearrange("(t p) c -> p t c", p=128))
        P.op("gpsimd", f, waits=sw, inc=s_slab[par], inc_by=16)
        if rider:
            def f2(e, s=s, par=par):
                return e.dma_start(out=brider[par][:, :], in_=b_all[:, s * OC:(s + 1) * OC])
            P.op("gpsimd", f2, inc=s_slab[par], inc_by=16)
        return par, s_slab[par].n

    first_qev = [True]
    first_rope = [True]
    pend_tp = [None]
    def emit_qtp(job):
        (s_, t_, qp_, c_rope_) = job
        tb, tw = qtp_ring.acquire()
        treg = tb[:, 0:256]
        for hh in range(2):
            P.op("tensor", lambda e, treg=treg, hh=hh, qp=qp_: e.matmul(
                    treg[:, hh * 128:(hh + 1) * 128], qm1[qp][:, hh * 128:(hh + 1) * 128],
                    identB[:, :], start=True, stop=False),
                 waits=(tw + [(s_rope, c_rope_)]) if hh == 0 else ())
            P.op("tensor", lambda e, treg=treg, hh=hh, qp=qp_: e.matmul(
                    treg[:, hh * 128:(hh + 1) * 128], qm2[qp][:, hh * 128:(hh + 1) * 128],
                    identB[:, :], start=False, stop=True),
                 inc=(s_qtp if hh == 1 else None))
        c_qtp = s_qtp.n
        qm_free[qp_] = [(s_qtp, c_qtp)]
        wev = [(s_qtp, c_qtp)]
        if first_qev[0]:
            wev.extend(B0_DONE)
            first_qev[0] = False
        def fe(e, treg=treg, s=s_, t=t_):
            dst = qrT[:, 2 * s:2 * s + 2, t * 128:(t + 1) * 128]
            return e.activation(out=dst, in_=treg, func=AF.Copy)
        P.op("scalar", fe, waits=wev, inc=s_qev)
        qtp_ring.release_last((s_qev, s_qev.n))

    for s in range(NSLAB):
        par, c_slab = emit_slab_load(wqT, s)
        for t in range(NT):
            bank, qw = q_ring.acquire()
            reg = bank[:, 0:256]
            for d in range(32):
                def f(e, d=d, t=t, reg=reg, par=par):
                    return e.matmul(reg, ln1T_own[:, d, t * 128:(t + 1) * 128],
                                    slabs[par][:, d, :], start=(d == 0), stop=(d == 31))
                P.op("tensor", f,
                     waits=(qw + [(s_slab[par], c_slab)]) if d == 0 else (),
                     inc=(s_qmm if d == 31 else None))
            c_q = s_qmm.n
            qp = (s * NT + t) & 1
            w_rope = [(s_qmm, c_q)] + qm_free[qp]
            if first_rope[0]:
                w_rope.append((s_c2, C2_N))
                first_rope[0] = False
            P.op("vector", lambda e, reg=reg, t=t, qp=qp: e.tensor_mul(
                    qm1[qp][:, :], reg, cos2_s[:, t, :]),
                 waits=w_rope, inc=s_rope)
            rv = lambda ap: ap.rearrange("p (h s j) -> p h s j", h=2, s=2)
            P.op("vector", lambda e, reg=reg, t=t, qp=qp, rv=rv: e.tensor_mul(
                    rv(qm2[qp][:, :])[:, :, 0, :], rv(reg)[:, :, 1, :],
                    rv(sinp2_s[:, t, :])[:, :, 0, :]))
            P.op("vector", lambda e, reg=reg, t=t, qp=qp, rv=rv: e.tensor_mul(
                    rv(qm2[qp][:, :])[:, :, 1, :], rv(reg)[:, :, 0, :],
                    rv(sinp2_s[:, t, :])[:, :, 1, :]), inc=s_rope)
            c_rope = s_rope.n
            q_ring.release_last((s_rope, c_rope))
            job = pend_tp[0]
            pend_tp[0] = (s, t, qp, c_rope)
            if job is not None:
                emit_qtp(job)
        slab_ring.release_last((s_qmm, s_qmm.n))
    emit_qtp(pend_tp[0])
    B_DONE = s_qev.n

    sc_ring = Ring([(0, ps[0], ps[1]), (1, ps[4], ps[5])], init_conds=[(s_qev, B_DONE)])
    av0_free = [[(s_qev, B_DONE)] for _ in range(2)]
    av1_free = [[(s_qev, B_DONE)] for _ in range(2)]
    sig_free = [[], []]
    bl2_ring = Ring(list(range(8)))
    bl1_free = {}
    first_sig = [True]
    first_u = [True]

    def emit_scores(c, hg):
        sp = hg & 1
        sbuf_sig = sig[sp]
        sig_waits = list(sig_free[sp])
        for j in range(6):
            (pid, bA, bB), bw = sc_ring.acquire()
            for k in range(4):
                h = hg * 4 + k
                bank = bA if k < 2 else bB
                def f(e, bank=bank, k=k, j=j, h=h, c=c):
                    return e.matmul(bank[:, (k & 1) * 256:((k & 1) + 1) * 256],
                                    krT[:, 256 * c + 128 * j: 256 * c + 128 * j + 128],
                                    qrT[:, h, 256 * c: 256 * c + 256],
                                    start=True, stop=True)
                P.op("tensor", f, waits=(bw if k == 0 else ()),
                     inc=(s_sc if k == 3 else None))
            c_sc = s_sc.n
            ws = [(s_sc, c_sc)]
            if j == 0:
                ws += sig_waits
            if first_sig[0]:
                ws.append((s_const, CONST_N))
                first_sig[0] = False
            wide = ps_all[:, (0 if pid == 0 else 2048):(1024 if pid == 0 else 3072)]
            def fs(e, wide=wide, j=j, c=c, sbuf_sig=sbuf_sig):
                return e.activation(out=sbuf_sig[:, j, :, :],
                                    in_=wide, func=AF.Sigmoid,
                                    scale=SCALE, bias=kbias_s[:, c * 6 + j:c * 6 + j + 1])
            P.op("scalar", fs, waits=ws, inc=s_sig)
            sc_ring.release_last((s_sig, s_sig.n))
        return s_sig.n

    def emit_rav_main(c, hg, c_sig_all):
        sp = hg & 1
        sbuf_sig = sig[sp]
        binfo = []
        w_first = [(s_sig, c_sig_all)]
        if first_u[0]:
            w_first.append((s_const, CONST_N))
            first_u[0] = False
        first = True
        for h4 in range(4):
            bk0 = 2 + (h4 & 1)
            bk1 = 6 + (h4 & 1)
            (rpid, rbank, _rb2), rbw = sc_ring.acquire()
            r_end = 0
            w0d = 0
            w1d = 0
            for win in (0, 1):
                bk = bk0 if win == 0 else bk1
                for qh in range(2):
                    reg0 = qh * 128
                    col = win * 2 + qh
                    for jj in range(4):
                        j = 2 * win + jj
                        w0 = ()
                        if first:
                            w0 = list(w_first) + list(av0_free[h4 & 1]) + rbw
                        elif win == 0 and qh == 0 and jj == 0:
                            w0 = list(av0_free[h4 & 1]) + rbw
                        elif win == 1 and qh == 0 and jj == 0:
                            w0 = list(av1_free[h4 & 1])
                        def fa(e, j=j, h4=h4, qh=qh, c=c, jj=jj, reg0=reg0, bk=bk, sbuf_sig=sbuf_sig):
                            return e.matmul(ps[bk][:, reg0:reg0 + 128],
                                            sbuf_sig[:, j, h4, qh * 128:(qh + 1) * 128],
                                            v_sb[:, 2 * c + j, :],
                                            start=(jj == 0), stop=(jj == 3))
                        P.op("tensor", fa, waits=w0,
                             inc=(s_av if (qh == 1 and jj == 3) else None))
                        rst = (win == 0 and qh == 0 and jj == 0)
                        rlast = (win == 1 and qh == 1 and jj == 3)
                        def fr(e, j=j, col=col, qh=qh, h4=h4, st=rst, lst=rlast,
                               sbuf_sig=sbuf_sig, rbank=rbank):
                            return e.matmul(rbank[:, col:col + 1],
                                            sbuf_sig[:, j, h4, qh * 128:(qh + 1) * 128],
                                            onesc[:, :], start=st, stop=lst)
                        P.op("tensor", fr, inc=(s_av if rlast else None))
                        if rlast:
                            r_end = s_av.n
                        first = False
                if win == 0:
                    w0d = s_av.n
            w1d = s_av.n
            uoff = h4 * 4
            P.op("vector", lambda e, rbank=rbank, uoff=uoff: e.reciprocal(
                    ur16[:, uoff:uoff + 4], rbank[:, 0:4]),
                 waits=[(s_av, r_end), (s_u, s_u.n)], inc=s_chain)
            c_rc = s_chain.n
            sc_ring.release_last((s_chain, c_rc))
            P.op("vector", lambda e, c=c, uoff=uoff: e.tensor_mul(
                    u16[:, uoff:uoff + 4], ur16[:, uoff:uoff + 4],
                    alpha_s[:, c * 16 + uoff: c * 16 + uoff + 4]),
                 waits=[(s_chain, c_rc)], inc=s_u)
            c_u = s_u.n
            for qh in range(2):
                colp = uoff + 0 * 2 + qh
                colc = uoff + 1 * 2 + qh
                bp = (h4 & 1) * 2 + qh
                def fb1(e, qh=qh, h4=h4, colp=colp, bp=bp, bk0=bk0):
                    return e.tensor_scalar(bl1[bp][:, :],
                                           ps[bk0][:, qh * 128:(qh + 1) * 128],
                                           u16[:, colp:colp + 1], None, ALU.mult)
                P.op("vector", fb1,
                     waits=[(s_u, c_u), (s_av, w0d)] + list(bl1_free.get(bp, [])),
                     inc=s_chain)
                c_t1 = s_chain.n
                p2, w2 = bl2_ring.acquire()
                def fb2(e, qh=qh, h4=h4, colc=colc, p2=p2, bp=bp, bk1=bk1):
                    return e.scalar_tensor_tensor(bl2[p2][:, :],
                                                  ps[bk1][:, qh * 128:(qh + 1) * 128],
                                                  u16[:, colc:colc + 1],
                                                  bl1[bp][:, :],
                                                  ALU.mult, ALU.add)
                P.op("vector", fb2,
                     waits=[(s_chain, c_t1), (s_av, w1d)] + w2, inc=s_bl)
                bl1_free[bp] = [(s_bl, s_bl.n)]
                binfo.append((qh, h4, p2, s_bl.n))
                if qh == 1:
                    av0_free[h4 & 1] = [(s_chain, c_t1)]
                    av1_free[h4 & 1] = [(s_bl, s_bl.n)]
        sig_free[sp] = [(s_av, s_av.n)]
        return (c, hg, binfo)

    def emit_trans(info):
        c, hg, binfo = info
        (tpid, tbA, tbB), tbw0 = sc_ring.acquire()
        for qh in range(2):
            tb = tbA if qh == 0 else tbB
            tbw = tbw0 if qh == 0 else []
            for idx, (qh_, h4, p2, c_b2) in enumerate([b for b in binfo if b[0] == qh]):
                def ft(e, tb=tb, h4=h4, p2=p2):
                    return e.matmul(tb[:, h4 * 128:(h4 + 1) * 128], bl2[p2][:, :],
                                    identB[:, :], start=True, stop=True)
                P.op("tensor", ft,
                     waits=(tbw + [(s_bl, c_b2)]) if idx == 0 else [(s_bl, c_b2)],
                     inc=(s_atp if idx == 3 else None))
            c_at = s_atp.n
            def fevt(e, tb=tb, hg=hg, c=c, qh=qh):
                dst = attn_T[:, hg * 4:hg * 4 + 4, 256 * c + qh * 128: 256 * c + qh * 128 + 128]
                return e.activation(out=dst, in_=tb[:, 0:512], func=AF.Copy)
            P.op("scalar", fevt, waits=[(s_atp, c_at)], inc=s_ate)
            if qh == 1:
                sc_ring.release_last((s_ate, s_ate.n))
        for sl_ in bl2_ring.slots:
            sl_[1] = [(s_atp, s_atp.n)]

    seq = [(c, hg) for c in range(4) for hg in range(8)]
    sig_done = {}
    sig_done[seq[0]] = emit_scores(*seq[0])
    sig_done[seq[1]] = emit_scores(*seq[1])
    for i, (c, hg) in enumerate(seq):
        info = emit_rav_main(c, hg, sig_done[(c, hg)])
        if i + 2 < len(seq):
            sig_done[seq[i + 2]] = emit_scores(*seq[i + 2])
        emit_trans(info)
    C_DONE = s_ate.n
    s_fw2 = P.sem("s_fw2")
    s_pool = P.sem("s_pool")
    s_mev = P.sem("s_mev")
    FW2_N = 32

    p_ring = Ring([ps[0], ps[1], ps[2], ps[3]], init_conds=[(s_ate, C_DONE)])
    pev_counts = {}
    for s in range(NSLAB):
        par, c_slab = emit_slab_load(woT, s)
        if s == 1:
            P.op("gpsimd", lambda e: e.dma_start(
                    out=a_all_s[:, :, :],
                    in_=a_all.rearrange("(t p) c -> p t c", p=128)),
                 waits=[(s_ate, C_DONE)], inc=s_fw2, inc_by=16)
            P.op("gpsimd", lambda e: e.dma_start(
                    out=routerT_s[:, :, :],
                    in_=routerT.rearrange("(t p) c -> p t c", p=128)),
                 inc=s_fw2, inc_by=16)
        for t in range(NT):
            bank, pw = p_ring.acquire()
            reg = bank[:, 0:256]
            for d in range(32):
                def f(e, d=d, t=t, reg=reg, par=par):
                    return e.matmul(reg, attn_T[:, d, t * 128:(t + 1) * 128],
                                    slabs[par][:, d, :], start=(d == 0), stop=(d == 31))
                P.op("tensor", f,
                     waits=(pw + [(s_slab[par], c_slab), (s_ate, C_DONE)]) if d == 0 else (),
                     inc=(s_pmm if d == 31 else None))
            c_p = s_pmm.n
            def fe(e, reg=reg, t=t, s=s):
                return e.activation(out=proj[:, t, s * OC:(s + 1) * OC], in_=reg, func=AF.Copy)
            P.op("scalar", fe, waits=[(s_pmm, c_p)], inc=s_pev)
            if s == NSLAB - 1:
                pev_counts[t] = s_pev.n
            p_ring.release_last((s_pev, s_pev.n))
        slab_ring.release_last((s_pmm, s_pmm.n))
    D_MM = s_pev.n

    m_free = []
    midg_free = {0: [], 1: []}
    mg_ring = Ring([ps[6], ps[7]], init_conds=[(s_ate, C_DONE)])
    first_w16 = [True]
    mgT_counts = {}
    pend_mg = [None]
    last_midg = [0, 0]
    m_free_l = [m_free]
    def emit_router(t):
        for d in range(32):
            def f(e, d=d, t=t):
                return e.matmul(ps[4][:, 0:128], ln2T[:, d, t * 128:(t + 1) * 128],
                                a_all_s[:, d, :], start=(d == 0), stop=(d == 31))
            P.op("tensor", f,
                 waits=(list(m_free_l[0]) + [(s_fw2, FW2_N)] + list(ln2t_counts[t])) if d == 0 else ())
            def f2(e, d=d, t=t):
                return e.matmul(ps[5][:, 0:8], ln2T[:, d, t * 128:(t + 1) * 128],
                                routerT_s[:, d, :], start=(d == 0), stop=(d == 31))
            P.op("tensor", f2, inc=(s_mmm if d == 31 else None))
        c_m = s_mmm.n
        tp_ = t & 1
        rsb = router_sb[tp_]
        P.op("scalar", lambda e, rsb=rsb: e.activation(out=rsb[:, 0:128], in_=ps[4][:, 0:128], func=AF.Copy),
             waits=[(s_mmm, c_m), (s_pool, last_midg[tp_])], inc=s_mev)
        P.op("scalar", lambda e, rsb=rsb: e.activation(out=rsb[:, 128:136], in_=ps[5][:, 0:8], func=AF.Copy),
             inc=s_mev)
        c_ev = s_mev.n
        m_free_l[0] = [(s_mev, c_ev)]
        def pstep(fn, waits=()):
            P.op("gpsimd", fn, waits=list(waits) + [(s_pool, s_pool.n)], inc=s_pool)
        def vstep(fn, waits=()):
            P.op("vector", fn, waits=list(waits) + [(s_chain, s_chain.n)], inc=s_chain)
        lg = rsb[:, 128:136]
        vstep(lambda e: e.reduce_max(m1r[:, :], lg, AX.X), waits=[(s_mev, c_ev)])
        vstep(lambda e: e.tensor_scalar(dlog[:, :], lg, m1r[:, :], None, ALU.subtract))
        vstep(lambda e: e.tensor_scalar(eqm[:, :], dlog[:, :], 0.0, None, ALU.is_equal))
        vstep(lambda e: e.scalar_tensor_tensor(dmask[:, :], eqm[:, :], -1e30, dlog[:, :], ALU.mult, ALU.add))
        vstep(lambda e: e.reduce_max(m2r[:, :], dmask[:, :], AX.X))
        c_m2 = s_chain.n
        pstep(lambda e: e.tensor_scalar(eqm[:, :], dlog[:, :], m2r[:, :], None, ALU.is_ge),
              waits=[(s_chain, c_m2)])
        c_ge = s_pool.n
        P.op("scalar", lambda e: e.activation(out=edl[:, :], in_=dlog[:, :], func=AF.Exp),
             waits=[(s_chain, c_m2)], inc=s_rstd)
        c_exp = s_rstd.n
        pstep(lambda e: e.tensor_mul(gate[:, :], edl[:, :], eqm[:, :]), waits=[(s_rstd, c_exp)])
        pstep(lambda e: e.tensor_add(t4r[:, 0:4], gate[:, 0:4], gate[:, 4:8]))
        pstep(lambda e: e.tensor_add(t4r[:, 0:2], t4r[:, 0:2], t4r[:, 2:4]))
        pstep(lambda e: e.tensor_add(gsr[:, :], t4r[:, 0:1], t4r[:, 1:2]))
        c_gs = s_pool.n
        vstep(lambda e: e.reciprocal(rgsr[:, :], gsr[:, :]), waits=[(s_pool, c_gs)])
        vstep(lambda e: e.tensor_scalar(w8[:, :], gate[:, :], rgsr[:, :], 2.0, ALU.mult, ALU.mult))
        c_w8 = s_chain.n
        for ee in range(8):
            w16 = [(s_chain, c_w8)]
            if first_w16[0]:
                w16.append((s_const, CONST_N))
                first_w16[0] = False
            pstep(lambda e, ee=ee: e.tensor_scalar(wex[:, ee * 16:(ee + 1) * 16], ones16_s[:, :],
                                                   w8[:, ee:ee + 1], None, ALU.mult),
                  waits=w16)
        mp = t & 1
        pstep(lambda e, mp=mp, rsb=rsb: e.tensor_mul(midg[mp][:, :], rsb[:, 0:128], wex[:, :]),
              waits=list(midg_free[mp]))
        last_midg[tp_] = s_pool.n
        pend_mg[0] = (t, mp, s_pool.n)

    def emit_mgT(job):
        t, mp, c_mg = job
        tb, tw = mg_ring.acquire()
        P.op("tensor", lambda e, tb=tb, mp=mp: e.matmul(tb[:, 0:128], midg[mp][:, :], identB[:, :],
                                                        start=True, stop=True),
             waits=tw + [(s_pool, c_mg)], inc=s_tp)
        c_t = s_tp.n
        midg_free[mp] = [(s_tp, c_t)]
        P.op("scalar", lambda e, tb=tb, t=t: e.activation(out=midgT[:, t, :], in_=tb[:, 0:128], func=AF.Copy),
             waits=[(s_tp, c_t)], inc=s_mgT)
        mg_ring.release_last((s_mgT, s_mgT.n))
        mgT_counts[t] = s_mgT.n
    ln2t_counts = [0] * NT
    x1b_free = [[], []]
    last_add = [0]
    s1_ap = {}
    s2_info = {}
    def dpost_s1(t):
        tp = t & 1
        def fx(e, t=t):
            return e.dma_start(out=xre[:, :], in_=x_ext[(t + 2) * 128:(t + 3) * 128, :])
        P.op("gpsimd", fx, waits=[(s_pev, pev_counts[t]), (s_chain, last_add[0])],
             inc=s_xre, inc_by=16)
        c_xre = s_xre.n
        c_ap = emit_ln_stats_apply(proj[:, t, :], x1b[tp][:, :],
                                   waits=[(s_pev, pev_counts[t])],
                                   extra_apply_waits=list(x1b_free[tp]))
        s1_ap[t] = (c_ap, c_xre)
    def dpost_s2(t):
        tp = t & 1
        c_ap, c_xre = s1_ap[t]
        P.op("vector", lambda e, tp=tp: e.tensor_add(x1b[tp][:, :], x1b[tp][:, :], xre[:, :]),
             waits=[(s_apply, c_ap), (s_xre, c_xre)], inc=s_chain)
        c_add = s_chain.n
        last_add[0] = c_add
        def fw_(e, t=t, tp=tp):
            return e.dma_start(out=x1_dram[t * 128:(t + 1) * 128, :], in_=x1b[tp][:, :])
        P.op("sync", fw_, waits=[(s_chain, c_add)], inc=s_x1w[tp], inc_by=16)
        c_ap2 = emit_ln_stats_apply(x1b[tp][:, :], proj[:, t, :],
                                    waits=[(s_chain, c_add)])
        x1b_free[tp] = [(s_apply, c_ap2), (s_x1w[tp], s_x1w[tp].n)]
        s2_info[t] = c_ap2
    def dpost_s3(t):
        c_ap2 = s2_info[t]
        def dvf(g, t=t):
            return ln2T[:, 4 * g:4 * g + 4, t * 128:(t + 1) * 128]
        ln2t_counts[t] = emit_transpose_tile(proj[:, t, :], dvf,
                                             src_waits=[(s_apply, c_ap2)])
        emit_router(t)
    for t in range(NT + 2):
        if 1 <= t <= NT:
            dpost_s2(t - 1)
        if t >= 2:
            prev_mg = pend_mg[0]
            dpost_s3(t - 2)
            if prev_mg is not None:
                emit_mgT(prev_mg)
        if t < NT:
            dpost_s1(t)
    emit_mgT(pend_mg[0])
    D_DONE = (s_x1w[0].n, s_x1w[1].n)
    E_DONE = [(s_tpe, s_tpe.n), (s_tpe2, s_tpe2.n)]



    y_ring = Ring([ps[2], ps[3]], init_conds=[(s_pev, D_MM)])
    yev_counts = {}
    for s in range(NSLAB):
        par, c_slab = emit_slab_load(baseT, s, rider=True)
        for t in range(NT):
            bank, yw = y_ring.acquire()
            reg = bank[:, 0:256]
            for d in range(32):
                def f(e, d=d, t=t, reg=reg, par=par):
                    return e.matmul(reg, ln2T[:, d, t * 128:(t + 1) * 128],
                                    slabs[par][:, d, :], start=(d == 0), stop=False)
                P.op("tensor", f,
                     waits=(yw + [(s_slab[par], c_slab), (s_mgT, mgT_counts[t])]) if d == 0 else ())
            def finc(e, t=t, reg=reg, par=par):
                return e.matmul(reg, midgT[:, t, :], brider[par][:, :],
                                start=False, stop=True)
            P.op("tensor", finc, inc=s_ymm)
            c_y = s_ymm.n
            def fe(e, reg=reg, t=t, s=s):
                return e.activation(out=y_nat[:, t, s * OC:(s + 1) * OC], in_=reg, func=AF.Copy)
            P.op("scalar", fe, waits=[(s_ymm, c_y)], inc=s_yev)
            if s == NSLAB - 1:
                yev_counts[t] = s_yev.n
            y_ring.release_last((s_yev, s_yev.n))
        slab_ring.release_last((s_ymm, s_ymm.n))
    F_MM = s_yev.n

    fp_ap = {}
    ost_free = [[], []]
    last_fadd = [0]
    def fpost_s1(t):
        tp = t & 1
        def fx1(e, t=t):
            return e.dma_start(out=x1re[:, :], in_=x1_dram[t * 128:(t + 1) * 128, :])
        P.op("sync", fx1, waits=[(s_yev, yev_counts[t]), (s_chain, last_fadd[0]),
                                 (s_x1w[0], D_DONE[0]), (s_x1w[1], D_DONE[1])], inc=s_x1r, inc_by=16)
        c_r = s_x1r.n
        c_ap = emit_ln_stats_apply(y_nat[:, t, :], outstage[tp][:, :],
                                   waits=[(s_yev, yev_counts[t])],
                                   extra_apply_waits=list(ost_free[tp]))
        fp_ap[t] = (c_ap, c_r)
    def fpost_s2(t):
        tp = t & 1
        c_ap, c_r = fp_ap[t]
        P.op("vector", lambda e, tp=tp: e.tensor_add(outstage[tp][:, :], outstage[tp][:, :], x1re[:, :]),
             waits=[(s_apply, c_ap), (s_x1r, c_r)], inc=s_chain)
        c_add = s_chain.n
        last_fadd[0] = c_add
        def fo(e, t=t, tp=tp):
            return e.dma_start(out=out_hbm[t * 128:(t + 1) * 128, :], in_=outstage[tp][:, :])
        P.op("sync", fo, waits=[(s_chain, c_add)], inc=s_out[tp], inc_by=16)
        ost_free[tp] = [(s_out[tp], s_out[tp].n)]
    for t in range(NT + 1):
        if t >= 1:
            fpost_s2(t - 1)
        if t < NT:
            fpost_s1(t)
    P.op("sync", lambda e: e.nop(), waits=[(s_out[0], s_out[0].n), (s_out[1], s_out[1].n)])

    P.emit()
    es.close()
    return nc


def _host_inputs(inputs):
    x = np.asarray(inputs["L0_attn_x"])
    Wq = np.asarray(inputs["L0_attn_Wq"]); Wk = np.asarray(inputs["L0_attn_Wk"])
    Wv = np.asarray(inputs["L0_attn_Wv"]); Wo = np.asarray(inputs["L0_attn_Wo"])
    baseW = np.asarray(inputs["L0_ffn_base_W"]); routerW = np.asarray(inputs["L0_ffn_router_W"])
    loraA = np.asarray(inputs["L0_ffn_lora_A"]); loraB = np.asarray(inputs["L0_ffn_lora_B"])

    wqT = np.ascontiguousarray(Wq.T)
    wkT = np.ascontiguousarray(Wk.T)
    wvT = np.ascontiguousarray(Wv.T)
    woT = np.ascontiguousarray(Wo.T)
    baseT = np.ascontiguousarray(baseW.T)
    routerT = np.ascontiguousarray(routerW.T)
    a_all = np.ascontiguousarray(loraA.transpose(1, 0, 2).reshape(D, 128))
    b_all = np.ascontiguousarray(loraB.reshape(128, D))
    cos, sin = _rope_cache()
    sinp = np.concatenate([-sin[:, :64], sin[:, 64:]], axis=1)
    identity = np.eye(128, dtype=np.float32)
    onescol = np.ones((128, 1), dtype=np.float32)
    ones16 = np.ones((128, 16), dtype=np.float32)
    lin = np.linspace(0.0, 1.0, 256, dtype=np.float32)

    in_maps = []
    for core in range(NCORES):
        b, q = core // 4, core % 4
        t0 = 1024 * q
        x_ext_a = np.zeros((TEXT, D), dtype=np.float32)
        lo = max(0, t0 - 256); hi = min(4096, t0 + 1280)
        x_ext_a[lo - (t0 - 256): hi - (t0 - 256)] = x[b, lo:hi]
        cos_own = cos[t0:t0 + TOWN]; sinp_own = sinp[t0:t0 + TOWN]
        cos2_a = np.ascontiguousarray(np.concatenate([cos_own, cos_own], axis=1))
        sinp2_a = np.ascontiguousarray(np.concatenate([sinp_own, sinp_own], axis=1))
        cosk_a = np.zeros((TEXT, DH), dtype=np.float32)
        sinkp_a = np.zeros((TEXT, DH), dtype=np.float32)
        cosk_a[lo - (t0 - 256): hi - (t0 - 256)] = cos[lo:hi]
        sinkp_a[lo - (t0 - 256): hi - (t0 - 256)] = sinp[lo:hi]
        kbias_a = np.zeros((128, 24), dtype=np.float32)
        if q == 3:
            kbias_a[:, 3 * 6 + 4] = -30000.0
            kbias_a[:, 3 * 6 + 5] = -30000.0
        alpha_a = np.zeros((128, 64), dtype=np.float32)
        for c in range(4):
            a = lin.copy()
            if q == 0 and c == 0:
                a = np.ones(256, dtype=np.float32)
            for h4 in range(4):
                for qh in range(2):
                    alpha_a[:, c * 16 + h4 * 4 + 0 * 2 + qh] = 1.0 - a[qh * 128:(qh + 1) * 128]
                    alpha_a[:, c * 16 + h4 * 4 + 1 * 2 + qh] = a[qh * 128:(qh + 1) * 128]
        in_maps.append({
            "x_ext": x_ext_a, "wqT": wqT, "wkT": wkT, "wvT": wvT, "woT": woT,
            "baseT": baseT, "routerT": routerT, "a_all": a_all, "b_all": b_all,
            "cos2": cos2_a, "sinp2": sinp2_a, "cosk": cosk_a, "sinkp": sinkp_a,
            "kbias": kbias_a, "alpha_exp": alpha_a, "identity": identity,
            "onescol": onescol, "ones16": ones16,
        })
    return in_maps


def kernel(**inputs):
    in_maps = _host_inputs(inputs)
    nc = build_program()
    res = run_bass_kernel_spmd(nc, in_maps, core_ids=list(range(NCORES)))
    out = np.empty((2, 4096, D), dtype=np.float32)
    for core in range(NCORES):
        b, q = core // 4, core % 4
        out[b, 1024 * q:1024 * (q + 1)] = res.results[core]["out"]
    return out


# version 21 (final)
# speedup vs baseline: 1.0203x; 1.0203x over previous
import math
import numpy as np

import concourse.bass as bass
import concourse.mybir as mybir
from concourse.bass_utils import run_bass_kernel_spmd

F32 = mybir.dt.float32
BF16 = mybir.dt.bfloat16
AF = mybir.ActivationFunctionType
ALU = mybir.AluOpType
AX = mybir.AxisListType

D = 4096
DH = 128
H = 32
TOWN = 1024
TEXT = 1536
NT = 8
NTX = 12
OC = 256
NSLAB = D // OC
EPS = 1e-5
SCALE = 1.0 / float(np.sqrt(DH))
NCORES = 8


def _rope_cache(S=4096, Dh=DH, base=10000.0):
    inv_freq = 1.0 / base ** (np.arange(0, Dh, 2, dtype=np.float32) / Dh)
    t = np.arange(S, dtype=np.float32)
    f = np.outer(t, inv_freq)
    cos = np.zeros((S, Dh), dtype=np.float32)
    sin = np.zeros((S, Dh), dtype=np.float32)
    cos[:, 0::2] = np.cos(f); cos[:, 1::2] = np.cos(f)
    sin[:, 0::2] = np.sin(f); sin[:, 1::2] = np.sin(f)
    return cos, sin


class Sem:
    __slots__ = ("h", "n", "name")
    def __init__(self, h, name):
        self.h = h; self.n = 0; self.name = name


ENGINES = ("sync", "gpsimd", "tensor", "vector", "scalar")


class Prog:
    def __init__(self, nc):
        self.nc = nc
        self.ops = {e: [] for e in ENGINES}

    def sem(self, name):
        return Sem(self.nc.alloc_semaphore(name=name), name)

    def op(self, eng, fn, waits=(), inc=None, inc_by=1):
        if inc is not None:
            inc.n += inc_by
        self.ops[eng].append((fn, tuple(waits), inc, inc_by))

    def emit(self):
        block = self.nc.Block()
        with block:
            for eng_name in ENGINES:
                oplist = self.ops[eng_name]
                def make(oplist):
                    def body(eng):
                        for fn, waits, inc, inc_by in oplist:
                            for s, v in waits:
                                if v > 0:
                                    eng.wait_ge(s.h, v)
                            inst = fn(eng)
                            if inc is not None:
                                if isinstance(inst, list):
                                    inst = inst[-1]
                                inst.then_inc(inc.h, inc_by)
                    return body
                getattr(block, eng_name)(make(oplist))


class Ring:
    def __init__(self, payloads, init_conds=None):
        self.slots = [[p, list(init_conds or [])] for p in payloads]
        self.i = 0
    def acquire(self):
        s = self.slots[self.i % len(self.slots)]
        self.i += 1
        return s[0], list(s[1])
    def release_last(self, *conds):
        s = self.slots[(self.i - 1) % len(self.slots)]
        s[1] = list(conds)


def build_program():
    nc = bass.Bass("TRN2", target_bir_lowering=False)

    def din(name, shape, dt=F32):
        return nc.dram_tensor(name, list(shape), dt, kind="ExternalInput").ap()
    x_ext = din("x_ext", (TEXT, D))
    wqT = din("wqT", (D, D))
    wkT = din("wkT", (D, DH))
    wvT = din("wvT", (D, DH))
    woT = din("woT", (D, D))
    baseT = din("baseT", (D, D))
    routerT = din("routerT", (D, 8))
    a_all = din("a_all", (D, 128))
    b_all = din("b_all", (128, D))
    cos2 = din("cos2", (TOWN, 2 * DH))
    sinp2 = din("sinp2", (TOWN, 2 * DH))
    cosk = din("cosk", (TEXT, DH))
    sinkp = din("sinkp", (TEXT, DH))
    kbias = din("kbias", (128, 24))
    alpha_exp = din("alpha_exp", (128, 64))
    identity = din("identity", (128, 128))
    onescol = din("onescol", (128, 1))
    ones16 = din("ones16", (128, 16))
    out_hbm = nc.dram_tensor("out", [TOWN, D], F32, kind="ExternalOutput").ap()
    x1_dram = nc.dram_tensor("x1_scratch", [TOWN, D], BF16, kind="Internal").ap()

    BASE = 16384 + 64
    CONST = BASE
    SLAB0 = BASE + 2560
    SLAB1 = SLAB0 + 16384
    BIG1 = SLAB1 + 16384
    BIG2 = BIG1 + 65536
    FREG = BIG2 + 65536

    def sb(name, shape, dt, off):
        return nc.alloc_sbuf_tensor_at(name, list(shape), dt, offset=off)

    identB  = sb("identB", (128, 128), BF16, CONST + 0)
    onesc   = sb("onesc", (128, 1), BF16, CONST + 256)
    kbias_s = sb("kbias_s", (128, 24), F32, CONST + 288)
    alpha_s = sb("alpha_s", (128, 64), F32, CONST + 384)
    ones16_s= sb("ones16_s", (128, 16), F32, CONST + 640)
    bn6 = [sb(f"bn6_{i}", (128, 8, 6), F32, CONST + 704 + 192 * i) for i in range(2)]
    mv  = [sb(f"mv_{i}", (128, 2), F32, CONST + 1088 + 32 * i) for i in range(2)]
    lnv = [sb(f"lnv_{i}", (128, 1), F32, CONST + 1152 + 32 * i) for i in range(2)]
    rstd= [sb(f"rstd_{i}", (128, 1), F32, CONST + 1216 + 32 * i) for i in range(2)]
    negmr=[sb(f"negmr_{i}", (128, 1), F32, CONST + 1280 + 32 * i) for i in range(2)]
    ur16 = sb("ur16", (128, 16), F32, CONST + 1344)
    u16  = sb("u16", (128, 16), F32, CONST + 1792)
    m1r  = sb("m1r", (128, 1), F32, CONST + 1440)
    m2r  = sb("m2r", (128, 1), F32, CONST + 1472)
    gsr  = sb("gsr", (128, 1), F32, CONST + 1504)
    rgsr = sb("rgsr", (128, 1), F32, CONST + 1536)
    dlog = sb("dlog", (128, 8), F32, CONST + 1568)
    eqm  = sb("eqm", (128, 8), F32, CONST + 1600)
    dmask= sb("dmask", (128, 8), F32, CONST + 1632)
    edl  = sb("edl", (128, 8), F32, CONST + 1664)
    gate = sb("gate", (128, 8), F32, CONST + 1696)
    w8   = sb("w8", (128, 8), F32, CONST + 1728)
    t4r  = sb("t4r", (128, 4), F32, CONST + 1760)

    slabs = [sb(f"slab{i}", (128, 32, OC), BF16, SLAB0 if i == 0 else SLAB1)
             for i in range(2)]

    ln1T_own = sb("ln1T_own", (128, 32, TOWN), BF16, BIG1)
    attn_T   = sb("attn_T", (128, 32, TOWN), BF16, BIG1)
    ln2T     = sb("ln2T", (128, 32, TOWN), BF16, BIG1)

    xstage = [sb(f"xstage{i}", (128, D), F32, BIG2 + 16384 * i) for i in range(2)]
    wkT_s  = sb("wkT_s", (128, 32, DH), BF16, BIG2 + 32768)
    wvT_s  = sb("wvT_s", (128, 32, DH), BF16, BIG2 + 40960)
    cosk_s = sb("cosk_s", (128, NTX, DH), F32, BIG2 + 49152)
    sinkp_s= sb("sinkp_s", (128, NTX, DH), F32, BIG2 + 55296)
    qrT    = sb("qrT", (128, 32, TOWN), BF16, BIG2)
    proj   = sb("proj", (128, NT, D), BF16, BIG2)
    y_nat  = sb("y_nat", (128, NT, D), BF16, BIG2)

    ln1T_halo = sb("ln1T_halo", (128, 32, 512), BF16, FREG)
    xt_stage  = sb("xt_stage", (128, D), BF16, FREG + 32768)
    v_sb      = sb("v_sb", (128, NTX, DH), BF16, FREG + 40320)
    krT       = sb("krT", (128, TEXT), BF16, FREG + 43392)
    cos2_s  = sb("cos2_s", (128, NT, 2 * DH), F32, FREG)
    sinp2_s = sb("sinp2_s", (128, NT, 2 * DH), F32, FREG + 8192)
    qm1 = [sb(f"qm1_{i}", (128, 2 * DH), BF16, FREG + 16384 + 512 * i) for i in range(2)]
    qm2 = [sb(f"qm2_{i}", (128, 2 * DH), BF16, FREG + 17408 + 512 * i) for i in range(2)]
    km1 = [sb(f"km1_{i}", (128, DH), BF16, FREG + 32768 + 256 * i) for i in range(2)]
    km2 = [sb(f"km2_{i}", (128, DH), BF16, FREG + 33280 + 256 * i) for i in range(2)]
    sig = [sb(f"sig_{i}", (128, 6, 4, 256), BF16, FREG + 12288 * i) for i in range(2)]
    bl1 = [sb(f"bl1_{i}", (128, 128), F32, FREG + 24576 + 512 * i) for i in range(4)]
    bl2 = [sb(f"bl2_{i}", (128, 128), BF16, FREG + 28672 + 256 * i) for i in range(8)]
    xre    = sb("xre", (128, D), BF16, FREG)
    x1b    = [sb(f"x1b_{i}", (128, D), BF16, FREG + 16384 + 8192 * i) for i in range(2)]
    outstage = [sb(f"outstage_{i}", (128, D), F32, FREG + 24576 * i) for i in range(2)]
    x1re     = sb("x1re", (128, D), BF16, FREG + 16384)
    a_all_s  = sb("a_all_s", (128, 32, 128), BF16, FREG + 32768)
    routerT_s= sb("routerT_s", (128, 32, 8), BF16, FREG + 40960)
    midgT    = sb("midgT", (128, NT, 128), BF16, FREG + 41472)
    midg  = [sb(f"midg_{i}", (128, 128), BF16, FREG + 45568 + 256 * i) for i in range(2)]
    router_sb = [sb(f"router_sb{i}", (128, 136), F32, FREG + 8192 + 544 * i + (32 - 544 % 32) * i) for i in range(2)]
    brider = [sb(f"brider_{i}", (128, OC), BF16, FREG + 43520 + 512 * i) for i in range(2)]
    wex      = sb("wex", (128, 128), F32, FREG + 44544)

    import contextlib
    es = contextlib.ExitStack()
    ps_all = es.enter_context(nc.psum_tensor("ps_all", [128, 4096], F32))
    ps = [ps_all[:, 512 * i:512 * (i + 1)] for i in range(8)]

    P = Prog(nc)
    s_xin   = [P.sem("s_xin0"), P.sem("s_xin1")]
    s_const = P.sem("s_const")
    s_wkv   = P.sem("s_wkv")
    s_slab  = [P.sem("s_slab0"), P.sem("s_slab1")]
    s_chain = P.sem("s_chain")
    s_rstd  = P.sem("s_rstd")
    s_negmr = P.sem("s_negmr")
    s_apply = P.sem("s_apply")
    s_tp    = P.sem("s_tp")
    s_tpe   = P.sem("s_tpe")
    s_tpe2  = P.sem("s_tpe2")
    s_kvmm  = P.sem("s_kvmm")
    s_rope  = P.sem("s_rope")
    s_vev   = P.sem("s_vev")
    s_qmm   = P.sem("s_qmm")
    s_qtp   = P.sem("s_qtp")
    s_qev   = P.sem("s_qev")
    s_sc    = P.sem("s_sc")
    s_sig   = P.sem("s_sig")
    s_av    = P.sem("s_av")
    s_u     = P.sem("s_u")
    s_bl    = P.sem("s_bl")
    s_atp   = P.sem("s_atp")
    s_ate   = P.sem("s_ate")
    s_pmm   = P.sem("s_pmm")
    s_pev   = P.sem("s_pev")
    s_xre   = P.sem("s_xre")
    s_x1w   = [P.sem("s_x1w0"), P.sem("s_x1w1")]
    s_x1r   = P.sem("s_x1r")
    s_mmm   = P.sem("s_mmm")
    s_mgT   = P.sem("s_mgT")
    s_ymm   = P.sem("s_ymm")
    s_yev   = P.sem("s_yev")
    s_out   = [P.sem("s_out0"), P.sem("s_out1")]
    s_fw    = P.sem("s_fw")
    s_gc    = P.sem("s_gc")

    P.op("sync", lambda e: e.dma_start(out=kbias_s[:, :], in_=kbias[:, :]), inc=s_const, inc_by=16)
    P.op("sync", lambda e: e.dma_start(out=alpha_s[:, :], in_=alpha_exp[:, :]), inc=s_const, inc_by=16)
    P.op("sync", lambda e: e.dma_start(out=ones16_s[:, :], in_=ones16[:, :]), inc=s_const, inc_by=16)
    P.op("sync", lambda e: e.dma_start(out=cosk_s[:, :, :],
                                       in_=cosk.rearrange("(t p) c -> p t c", p=128)), inc=s_const, inc_by=16)
    P.op("sync", lambda e: e.dma_start(out=sinkp_s[:, :, :],
                                       in_=sinkp.rearrange("(t p) c -> p t c", p=128)), inc=s_const, inc_by=16)
    P.op("gpsimd", lambda e: e.dma_start(out=identB[:, :], in_=identity[:, :]), inc=s_gc, inc_by=16)
    P.op("gpsimd", lambda e: e.dma_start(out=onesc[:, :], in_=onescol[:, :]), inc=s_gc, inc_by=16)
    P.op("gpsimd", lambda e: e.dma_start(out=wkT_s[:, :, :],
                                         in_=wkT.rearrange("(t p) c -> p t c", p=128)), inc=s_wkv, inc_by=16)
    P.op("gpsimd", lambda e: e.dma_start(out=wvT_s[:, :, :],
                                         in_=wvT.rearrange("(t p) c -> p t c", p=128)), inc=s_wkv, inc_by=16)
    CONST_N = s_const.n
    GC_N = s_gc.n
    WKV_N = s_wkv.n

    ln_par = [0]
    ln_free = {0: 0, 1: 0}
    def emit_ln_stats_apply(src_ap, dst_ap, waits, extra_apply_waits=()):
        par = ln_par[0] & 1
        ln_par[0] += 1
        b6, m, lv, rs, nm = bn6[par], mv[par], lnv[par], rstd[par], negmr[par]
        w0 = list(waits) + [(s_apply, ln_free[par])]
        for j in range(8):
            def f(e, j=j, b6=b6, src=src_ap):
                return e.bn_stats(b6[:, j, :], src[:, j * 512:(j + 1) * 512])
            P.op("vector", f, waits=(w0 if j == 0 else ()),
                 inc=(s_chain if j == 7 else None))
        c_bn = s_chain.n
        P.op("vector", lambda e, b6=b6, m=m: e.bn_aggr(m[:, :], b6[:, :, :]),
             waits=[(s_chain, c_bn)], inc=s_chain)
        c_mv = s_chain.n
        P.op("vector", lambda e, m=m, lv=lv: e.tensor_scalar(lv[:, :], m[:, 1:2], EPS, None, ALU.add),
             waits=[(s_chain, c_mv)], inc=s_chain)
        c_ve = s_chain.n
        P.op("scalar", lambda e, lv=lv, lv2=lv: e.activation(out=lv[:, :], in_=lv[:, :], func=AF.Ln),
             waits=[(s_chain, c_ve)], inc=s_rstd)
        c_ln = s_rstd.n
        P.op("scalar", lambda e, lv=lv, rs=rs: e.activation(out=rs[:, :], in_=lv[:, :], func=AF.Exp, scale=-0.5),
             waits=[(s_rstd, c_ln)], inc=s_rstd)
        c_rs = s_rstd.n
        P.op("vector", lambda e, m=m, rs=rs, nm=nm: e.tensor_scalar(nm[:, :], m[:, 0:1], rs[:, :], -1.0, ALU.mult, ALU.mult),
             waits=[(s_rstd, c_rs)], inc=s_negmr)
        c_nm = s_negmr.n
        P.op("scalar", lambda e, src=src_ap, dst=dst_ap, rs=rs, nm=nm: e.activation(
                out=dst, in_=src, func=AF.Identity, scale=rs[:, :], bias=nm[:, :]),
             waits=[(s_negmr, c_nm)] + list(extra_apply_waits), inc=s_apply)
        ln_free[par] = s_apply.n
        return s_apply.n

    tpb_ring = Ring([ps[0], ps[1]])
    def emit_transpose_tile(src_tile_ap, dst_view_fn, src_waits, evac_eng="scalar"):
        for g in range(8):
            bank, bwaits = tpb_ring.acquire()
            for k in range(4):
                dcol = 4 * g + k
                def f(e, bank=bank, k=k, dcol=dcol, src=src_tile_ap):
                    return e.matmul(bank[:, k * 128:(k + 1) * 128],
                                    src[:, dcol * 128:(dcol + 1) * 128],
                                    identB[:, :], start=True, stop=True)
                P.op("tensor", f,
                     waits=(list(src_waits) + bwaits + [(s_gc, GC_N)]) if k == 0 else (),
                     inc=(s_tp if k == 3 else None))
            c_tp = s_tp.n
            eng = ("scalar" if (g & 1) == 0 else "vector") if evac_eng == "alt" else evac_eng
            if eng == "scalar":
                def fe(e, bank=bank, g=g):
                    return e.activation(out=dst_view_fn(g), in_=bank[:, :], func=AF.Copy)
            else:
                def fe(e, bank=bank, g=g):
                    return e.tensor_copy(dst_view_fn(g), bank[:, :])
            sem = s_tpe if eng == "scalar" else s_tpe2
            P.op(eng, fe, waits=[(s_tp, c_tp)], inc=sem)
            tpb_ring.release_last((sem, sem.n))
        return [(s_tpe, s_tpe.n), (s_tpe2, s_tpe2.n)]

    xs_ring = Ring([0, 1])
    a_evac_counts = [0] * NTX

    def ln1T_view(dt, ti):
        if 2 <= ti < 10:
            return ln1T_own[:, dt, (ti - 2) * 128:(ti - 1) * 128]
        hi = {0: 0, 1: 1, 10: 2, 11: 3}[ti]
        return ln1T_halo[:, dt, hi * 128:(hi + 1) * 128]

    def emit_a(ti):
        par, xw = xs_ring.acquire()
        xb = xstage[par]
        def fl(e, ti=ti, xb=xb):
            return e.dma_start(out=xb[:, :], in_=x_ext[ti * 128:(ti + 1) * 128, :])
        P.op("sync", fl, waits=xw, inc=s_xin[par], inc_by=16)
        c_x = s_xin[par].n
        c_ap = emit_ln_stats_apply(xb[:, :], xt_stage[:, :],
                                   waits=[(s_xin[par], c_x)],
                                   extra_apply_waits=[(s_tpe, s_tpe.n), (s_tpe2, s_tpe2.n)])
        xs_ring.release_last((s_apply, c_ap))
        if 2 <= ti < 10:
            tloc = ti - 2
            def dvf(g, tloc=tloc):
                return ln1T_own[:, 4 * g:4 * g + 4, tloc * 128:(tloc + 1) * 128]
        else:
            hi = {0: 0, 1: 1, 10: 2, 11: 3}[ti]
            def dvf(g, hi=hi):
                return ln1T_halo[:, 4 * g:4 * g + 4, hi * 128:(hi + 1) * 128]
        a_evac_counts[ti] = emit_transpose_tile(xt_stage[:, :], dvf,
                                                src_waits=[(s_apply, c_ap)])

    kv_ring = Ring([(ps[2], ps[3]), (ps[4], ps[5])])
    ktp_ring = Ring([ps[6], ps[7]])

    def emit_b0(ti):
        (kb, vb), kvw = kv_ring.acquire()
        for d in range(32):
            def f(e, d=d, ti=ti, kb=kb):
                return e.matmul(kb[:, 0:128], ln1T_view(d, ti), wkT_s[:, d, :],
                                start=(d == 0), stop=(d == 31))
            P.op("tensor", f,
                 waits=(kvw + [(s_wkv, WKV_N)] + a_evac_counts[ti]) if d == 0 else ())
            def f2(e, d=d, ti=ti, vb=vb):
                return e.matmul(vb[:, 0:128], ln1T_view(d, ti), wvT_s[:, d, :],
                                start=(d == 0), stop=(d == 31))
            P.op("tensor", f2, inc=(s_kvmm if d == 31 else None))
        c_kv = s_kvmm.n
        kp = ti & 1
        P.op("vector", lambda e, kb=kb, ti=ti, kp=kp: e.tensor_mul(
                km1[kp][:, :], kb[:, 0:128], cosk_s[:, ti, :]),
             waits=[(s_kvmm, c_kv), (s_const, CONST_N), (s_tp, s_tp.n)], inc=s_rope)
        P.op("vector", lambda e, kb=kb, ti=ti, kp=kp: e.tensor_mul(
                km2[kp][:, 0:64], kb[:, 64:128], sinkp_s[:, ti, 0:64]))
        P.op("vector", lambda e, kb=kb, ti=ti, kp=kp: e.tensor_mul(
                km2[kp][:, 64:128], kb[:, 0:64], sinkp_s[:, ti, 64:128]), inc=s_rope)
        c_rope = s_rope.n
        P.op("scalar", lambda e, vb=vb, ti=ti: e.activation(
                out=v_sb[:, ti, :], in_=vb[:, 0:128], func=AF.Copy),
             waits=[(s_kvmm, c_kv)], inc=s_vev)
        kv_ring.release_last((s_rope, c_rope), (s_vev, s_vev.n))
        tb, tw = ktp_ring.acquire()
        P.op("tensor", lambda e, tb=tb, kp=kp: e.matmul(
                tb[:, 0:128], km1[kp][:, :], identB[:, :], start=True, stop=False),
             waits=tw + [(s_rope, c_rope)])
        P.op("tensor", lambda e, tb=tb, kp=kp: e.matmul(
                tb[:, 0:128], km2[kp][:, :], identB[:, :], start=False, stop=True), inc=s_tp)
        c_ktp = s_tp.n
        P.op("scalar", lambda e, tb=tb, ti=ti: e.activation(
                out=krT[:, ti * 128:(ti + 1) * 128], in_=tb[:, 0:128], func=AF.Copy),
             waits=[(s_tp, c_ktp)], inc=s_tpe)
        ktp_ring.release_last((s_tpe, s_tpe.n))

    for ti in range(NTX):
        emit_a(ti)
    for ti in range(NTX):
        emit_b0(ti)
    B0_DONE = [(s_tpe, s_tpe.n), (s_tpe2, s_tpe2.n)]
    B0_KV = s_kvmm.n
    s_c2 = P.sem("s_c2")
    P.op("sync", lambda e: e.dma_start(out=cos2_s[:, :, :],
                                       in_=cos2.rearrange("(t p) c -> p t c", p=128)),
         waits=[(s_kvmm, B0_KV)], inc=s_c2, inc_by=16)
    P.op("sync", lambda e: e.dma_start(out=sinp2_s[:, :, :],
                                       in_=sinp2.rearrange("(t p) c -> p t c", p=128)),
         waits=[(s_kvmm, B0_KV)], inc=s_c2, inc_by=16)
    C2_N = s_c2.n

    slab_ring = Ring([0, 1])
    q_ring = Ring([ps[0], ps[1], ps[2], ps[3]],
                  init_conds=list(B0_DONE))
    qtp_ring = Ring([ps[4], ps[5]], init_conds=list(B0_DONE))
    qm_free = {0: [], 1: []}

    def emit_slab_load(src, s, rider=False):
        par, sw = slab_ring.acquire()
        def f(e, src=src, s=s, par=par):
            return e.dma_start(
                out=slabs[par][:, :, :],
                in_=src[:, s * OC:(s + 1) * OC].rearrange("(t p) c -> p t c", p=128))
        P.op("gpsimd", f, waits=sw, inc=s_slab[par], inc_by=16)
        if rider:
            def f2(e, s=s, par=par):
                return e.dma_start(out=brider[par][:, :], in_=b_all[:, s * OC:(s + 1) * OC])
            P.op("gpsimd", f2, inc=s_slab[par], inc_by=16)
        return par, s_slab[par].n

    first_qev = [True]
    first_rope = [True]
    pend_tp = [None]
    def emit_qtp(job):
        (s_, t_, qp_, c_rope_) = job
        tb, tw = qtp_ring.acquire()
        treg = tb[:, 0:256]
        for hh in range(2):
            P.op("tensor", lambda e, treg=treg, hh=hh, qp=qp_: e.matmul(
                    treg[:, hh * 128:(hh + 1) * 128], qm1[qp][:, hh * 128:(hh + 1) * 128],
                    identB[:, :], start=True, stop=False),
                 waits=(tw + [(s_rope, c_rope_)]) if hh == 0 else ())
            P.op("tensor", lambda e, treg=treg, hh=hh, qp=qp_: e.matmul(
                    treg[:, hh * 128:(hh + 1) * 128], qm2[qp][:, hh * 128:(hh + 1) * 128],
                    identB[:, :], start=False, stop=True),
                 inc=(s_qtp if hh == 1 else None))
        c_qtp = s_qtp.n
        qm_free[qp_] = [(s_qtp, c_qtp)]
        wev = [(s_qtp, c_qtp)]
        if first_qev[0]:
            wev.extend(B0_DONE)
            first_qev[0] = False
        def fe(e, treg=treg, s=s_, t=t_):
            dst = qrT[:, 2 * s:2 * s + 2, t * 128:(t + 1) * 128]
            return e.activation(out=dst, in_=treg, func=AF.Copy)
        P.op("scalar", fe, waits=wev, inc=s_qev)
        qtp_ring.release_last((s_qev, s_qev.n))

    for s in range(NSLAB):
        par, c_slab = emit_slab_load(wqT, s)
        for t in range(NT):
            bank, qw = q_ring.acquire()
            reg = bank[:, 0:256]
            for d in range(32):
                def f(e, d=d, t=t, reg=reg, par=par):
                    return e.matmul(reg, ln1T_own[:, d, t * 128:(t + 1) * 128],
                                    slabs[par][:, d, :], start=(d == 0), stop=(d == 31))
                P.op("tensor", f,
                     waits=(qw + [(s_slab[par], c_slab)]) if d == 0 else (),
                     inc=(s_qmm if d == 31 else None))
            c_q = s_qmm.n
            qp = (s * NT + t) & 1
            w_rope = [(s_qmm, c_q)] + qm_free[qp]
            if first_rope[0]:
                w_rope.append((s_c2, C2_N))
                first_rope[0] = False
            P.op("vector", lambda e, reg=reg, t=t, qp=qp: e.tensor_mul(
                    qm1[qp][:, :], reg, cos2_s[:, t, :]),
                 waits=w_rope, inc=s_rope)
            rv = lambda ap: ap.rearrange("p (h s j) -> p h s j", h=2, s=2)
            P.op("vector", lambda e, reg=reg, t=t, qp=qp, rv=rv: e.tensor_mul(
                    rv(qm2[qp][:, :])[:, :, 0, :], rv(reg)[:, :, 1, :],
                    rv(sinp2_s[:, t, :])[:, :, 0, :]))
            P.op("vector", lambda e, reg=reg, t=t, qp=qp, rv=rv: e.tensor_mul(
                    rv(qm2[qp][:, :])[:, :, 1, :], rv(reg)[:, :, 0, :],
                    rv(sinp2_s[:, t, :])[:, :, 1, :]), inc=s_rope)
            c_rope = s_rope.n
            q_ring.release_last((s_rope, c_rope))
            job = pend_tp[0]
            pend_tp[0] = (s, t, qp, c_rope)
            if job is not None:
                emit_qtp(job)
        slab_ring.release_last((s_qmm, s_qmm.n))
    emit_qtp(pend_tp[0])
    B_DONE = s_qev.n

    sc_ring = Ring([ps[0], ps[1], ps[4], ps[5]], init_conds=[(s_qev, B_DONE)])
    av0_free = [[(s_qev, B_DONE)] for _ in range(2)]
    av1_free = [[(s_qev, B_DONE)] for _ in range(2)]
    sig_free = [[], []]
    bl2_ring = Ring(list(range(8)))
    bl1_free = {}
    first_sig = [True]
    first_u = [True]

    def emit_scores(c, hg):
        sp = hg & 1
        sbuf_sig = sig[sp]
        sig_waits = list(sig_free[sp])
        for j in range(6):
            for pair in range(2):
                bank, bw = sc_ring.acquire()
                for k in range(2):
                    h = hg * 4 + pair * 2 + k
                    def f(e, bank=bank, k=k, j=j, h=h, c=c):
                        return e.matmul(bank[:, k * 256:(k + 1) * 256],
                                        krT[:, 256 * c + 128 * j: 256 * c + 128 * j + 128],
                                        qrT[:, h, 256 * c: 256 * c + 256],
                                        start=True, stop=True)
                    P.op("tensor", f, waits=(bw if k == 0 else ()),
                         inc=(s_sc if k == 1 else None))
                c_sc = s_sc.n
                ws = [(s_sc, c_sc)]
                if j == 0 and pair == 0:
                    ws += sig_waits
                if first_sig[0]:
                    ws.append((s_const, CONST_N))
                    first_sig[0] = False
                def fs(e, bank=bank, j=j, pair=pair, c=c, sbuf_sig=sbuf_sig):
                    return e.activation(out=sbuf_sig[:, j, pair * 2:(pair + 1) * 2, :],
                                        in_=bank[:, :], func=AF.Sigmoid,
                                        scale=SCALE, bias=kbias_s[:, c * 6 + j:c * 6 + j + 1])
                P.op("scalar", fs, waits=ws, inc=s_sig)
                sc_ring.release_last((s_sig, s_sig.n))
        return s_sig.n

    def emit_rav_main(c, hg, c_sig_all):
        sp = hg & 1
        sbuf_sig = sig[sp]
        binfo = []
        w_first = [(s_sig, c_sig_all)]
        if first_u[0]:
            w_first.append((s_const, CONST_N))
            first_u[0] = False
        first = True
        for h4 in range(4):
            bk0 = 2 + (h4 & 1)
            bk1 = 6 + (h4 & 1)
            rbank, rbw = sc_ring.acquire()
            r_end = 0
            w0d = 0
            w1d = 0
            for win in (0, 1):
                bk = bk0 if win == 0 else bk1
                for qh in range(2):
                    reg0 = qh * 128
                    col = win * 2 + qh
                    for jj in range(4):
                        j = 2 * win + jj
                        w0 = ()
                        if first:
                            w0 = list(w_first) + list(av0_free[h4 & 1]) + rbw
                        elif win == 0 and qh == 0 and jj == 0:
                            w0 = list(av0_free[h4 & 1]) + rbw
                        elif win == 1 and qh == 0 and jj == 0:
                            w0 = list(av1_free[h4 & 1])
                        def fa(e, j=j, h4=h4, qh=qh, c=c, jj=jj, reg0=reg0, bk=bk, sbuf_sig=sbuf_sig):
                            return e.matmul(ps[bk][:, reg0:reg0 + 128],
                                            sbuf_sig[:, j, h4, qh * 128:(qh + 1) * 128],
                                            v_sb[:, 2 * c + j, :],
                                            start=(jj == 0), stop=(jj == 3))
                        P.op("tensor", fa, waits=w0,
                             inc=(s_av if (qh == 1 and jj == 3) else None))
                        rst = (win == 0 and qh == 0 and jj == 0)
                        rlast = (win == 1 and qh == 1 and jj == 3)
                        def fr(e, j=j, col=col, qh=qh, h4=h4, st=rst, lst=rlast,
                               sbuf_sig=sbuf_sig, rbank=rbank):
                            return e.matmul(rbank[:, col:col + 1],
                                            sbuf_sig[:, j, h4, qh * 128:(qh + 1) * 128],
                                            onesc[:, :], start=st, stop=lst)
                        P.op("tensor", fr, inc=(s_av if rlast else None))
                        if rlast:
                            r_end = s_av.n
                        first = False
                if win == 0:
                    w0d = s_av.n
            w1d = s_av.n
            uoff = h4 * 4
            P.op("vector", lambda e, rbank=rbank, uoff=uoff: e.reciprocal(
                    ur16[:, uoff:uoff + 4], rbank[:, 0:4]),
                 waits=[(s_av, r_end), (s_u, s_u.n)], inc=s_chain)
            c_rc = s_chain.n
            sc_ring.release_last((s_chain, c_rc))
            P.op("vector", lambda e, c=c, uoff=uoff: e.tensor_mul(
                    u16[:, uoff:uoff + 4], ur16[:, uoff:uoff + 4],
                    alpha_s[:, c * 16 + uoff: c * 16 + uoff + 4]),
                 waits=[(s_chain, c_rc)], inc=s_u)
            c_u = s_u.n
            for qh in range(2):
                colp = uoff + 0 * 2 + qh
                colc = uoff + 1 * 2 + qh
                bp = (h4 & 1) * 2 + qh
                def fb1(e, qh=qh, h4=h4, colp=colp, bp=bp, bk0=bk0):
                    return e.tensor_scalar(bl1[bp][:, :],
                                           ps[bk0][:, qh * 128:(qh + 1) * 128],
                                           u16[:, colp:colp + 1], None, ALU.mult)
                P.op("vector", fb1,
                     waits=[(s_u, c_u), (s_av, w0d)] + list(bl1_free.get(bp, [])),
                     inc=s_chain)
                c_t1 = s_chain.n
                p2, w2 = bl2_ring.acquire()
                def fb2(e, qh=qh, h4=h4, colc=colc, p2=p2, bp=bp, bk1=bk1):
                    return e.scalar_tensor_tensor(bl2[p2][:, :],
                                                  ps[bk1][:, qh * 128:(qh + 1) * 128],
                                                  u16[:, colc:colc + 1],
                                                  bl1[bp][:, :],
                                                  ALU.mult, ALU.add)
                P.op("vector", fb2,
                     waits=[(s_chain, c_t1), (s_av, w1d)] + w2, inc=s_bl)
                bl1_free[bp] = [(s_bl, s_bl.n)]
                binfo.append((qh, h4, p2, s_bl.n))
                if qh == 1:
                    av0_free[h4 & 1] = [(s_chain, c_t1)]
                    av1_free[h4 & 1] = [(s_bl, s_bl.n)]
        sig_free[sp] = [(s_av, s_av.n)]
        return (c, hg, binfo)

    def emit_trans(info):
        c, hg, binfo = info
        for qh in range(2):
            tb, tbw = sc_ring.acquire()
            for idx, (qh_, h4, p2, c_b2) in enumerate([b for b in binfo if b[0] == qh]):
                def ft(e, tb=tb, h4=h4, p2=p2):
                    return e.matmul(tb[:, h4 * 128:(h4 + 1) * 128], bl2[p2][:, :],
                                    identB[:, :], start=True, stop=True)
                P.op("tensor", ft,
                     waits=(tbw + [(s_bl, c_b2)]) if idx == 0 else [(s_bl, c_b2)],
                     inc=(s_atp if idx == 3 else None))
            c_at = s_atp.n
            def fevt(e, tb=tb, hg=hg, c=c, qh=qh):
                dst = attn_T[:, hg * 4:hg * 4 + 4, 256 * c + qh * 128: 256 * c + qh * 128 + 128]
                return e.activation(out=dst, in_=tb[:, 0:512], func=AF.Copy)
            P.op("scalar", fevt, waits=[(s_atp, c_at)], inc=s_ate)
            sc_ring.release_last((s_ate, s_ate.n))
        for sl_ in bl2_ring.slots:
            sl_[1] = [(s_atp, s_atp.n)]

    seq = [(c, hg) for c in range(4) for hg in range(8)]
    sig_done = {}
    sig_done[seq[0]] = emit_scores(*seq[0])
    sig_done[seq[1]] = emit_scores(*seq[1])
    for i, (c, hg) in enumerate(seq):
        info = emit_rav_main(c, hg, sig_done[(c, hg)])
        if i + 2 < len(seq):
            sig_done[seq[i + 2]] = emit_scores(*seq[i + 2])
        emit_trans(info)
    C_DONE = s_ate.n
    s_fw2 = P.sem("s_fw2")
    s_pool = P.sem("s_pool")
    s_mev = P.sem("s_mev")
    FW2_N = 32

    p_ring = Ring([ps[0], ps[1], ps[2], ps[3]], init_conds=[(s_ate, C_DONE)])
    pev_counts = {}
    for s in range(NSLAB):
        par, c_slab = emit_slab_load(woT, s)
        if s == 1:
            P.op("gpsimd", lambda e: e.dma_start(
                    out=a_all_s[:, :, :],
                    in_=a_all.rearrange("(t p) c -> p t c", p=128)),
                 waits=[(s_ate, C_DONE)], inc=s_fw2, inc_by=16)
            P.op("gpsimd", lambda e: e.dma_start(
                    out=routerT_s[:, :, :],
                    in_=routerT.rearrange("(t p) c -> p t c", p=128)),
                 inc=s_fw2, inc_by=16)
        for t in range(NT):
            bank, pw = p_ring.acquire()
            reg = bank[:, 0:256]
            for d in range(32):
                def f(e, d=d, t=t, reg=reg, par=par):
                    return e.matmul(reg, attn_T[:, d, t * 128:(t + 1) * 128],
                                    slabs[par][:, d, :], start=(d == 0), stop=(d == 31))
                P.op("tensor", f,
                     waits=(pw + [(s_slab[par], c_slab), (s_ate, C_DONE)]) if d == 0 else (),
                     inc=(s_pmm if d == 31 else None))
            c_p = s_pmm.n
            def fe(e, reg=reg, t=t, s=s):
                return e.activation(out=proj[:, t, s * OC:(s + 1) * OC], in_=reg, func=AF.Copy)
            P.op("scalar", fe, waits=[(s_pmm, c_p)], inc=s_pev)
            if s == NSLAB - 1:
                pev_counts[t] = s_pev.n
            p_ring.release_last((s_pev, s_pev.n))
        slab_ring.release_last((s_pmm, s_pmm.n))
    D_MM = s_pev.n

    m_free = []
    midg_free = {0: [], 1: []}
    mg_ring = Ring([ps[6], ps[7]], init_conds=[(s_ate, C_DONE)])
    first_w16 = [True]
    mgT_counts = {}
    pend_mg = [None]
    last_midg = [0, 0]
    m_free_l = [m_free]
    def emit_router(t):
        for d in range(32):
            def f(e, d=d, t=t):
                return e.matmul(ps[4][:, 0:128], ln2T[:, d, t * 128:(t + 1) * 128],
                                a_all_s[:, d, :], start=(d == 0), stop=(d == 31))
            P.op("tensor", f,
                 waits=(list(m_free_l[0]) + [(s_fw2, FW2_N)] + list(ln2t_counts[t])) if d == 0 else ())
            def f2(e, d=d, t=t):
                return e.matmul(ps[5][:, 0:8], ln2T[:, d, t * 128:(t + 1) * 128],
                                routerT_s[:, d, :], start=(d == 0), stop=(d == 31))
            P.op("tensor", f2, inc=(s_mmm if d == 31 else None))
        c_m = s_mmm.n
        tp_ = t & 1
        rsb = router_sb[tp_]
        P.op("scalar", lambda e, rsb=rsb: e.activation(out=rsb[:, 0:128], in_=ps[4][:, 0:128], func=AF.Copy),
             waits=[(s_mmm, c_m), (s_pool, last_midg[tp_])], inc=s_mev)
        P.op("scalar", lambda e, rsb=rsb: e.activation(out=rsb[:, 128:136], in_=ps[5][:, 0:8], func=AF.Copy),
             inc=s_mev)
        c_ev = s_mev.n
        m_free_l[0] = [(s_mev, c_ev)]
        def pstep(fn, waits=()):
            P.op("gpsimd", fn, waits=list(waits) + [(s_pool, s_pool.n)], inc=s_pool)
        def vstep(fn, waits=()):
            P.op("vector", fn, waits=list(waits) + [(s_chain, s_chain.n)], inc=s_chain)
        lg = rsb[:, 128:136]
        vstep(lambda e: e.reduce_max(m1r[:, :], lg, AX.X), waits=[(s_mev, c_ev)])
        vstep(lambda e: e.tensor_scalar(dlog[:, :], lg, m1r[:, :], None, ALU.subtract))
        vstep(lambda e: e.tensor_scalar(eqm[:, :], dlog[:, :], 0.0, None, ALU.is_equal))
        vstep(lambda e: e.scalar_tensor_tensor(dmask[:, :], eqm[:, :], -1e30, dlog[:, :], ALU.mult, ALU.add))
        vstep(lambda e: e.reduce_max(m2r[:, :], dmask[:, :], AX.X))
        c_m2 = s_chain.n
        pstep(lambda e: e.tensor_scalar(eqm[:, :], dlog[:, :], m2r[:, :], None, ALU.is_ge),
              waits=[(s_chain, c_m2)])
        c_ge = s_pool.n
        P.op("scalar", lambda e: e.activation(out=edl[:, :], in_=dlog[:, :], func=AF.Exp),
             waits=[(s_chain, c_m2)], inc=s_rstd)
        c_exp = s_rstd.n
        pstep(lambda e: e.tensor_mul(gate[:, :], edl[:, :], eqm[:, :]), waits=[(s_rstd, c_exp)])
        pstep(lambda e: e.tensor_add(t4r[:, 0:4], gate[:, 0:4], gate[:, 4:8]))
        pstep(lambda e: e.tensor_add(t4r[:, 0:2], t4r[:, 0:2], t4r[:, 2:4]))
        pstep(lambda e: e.tensor_add(gsr[:, :], t4r[:, 0:1], t4r[:, 1:2]))
        c_gs = s_pool.n
        vstep(lambda e: e.reciprocal(rgsr[:, :], gsr[:, :]), waits=[(s_pool, c_gs)])
        vstep(lambda e: e.tensor_scalar(w8[:, :], gate[:, :], rgsr[:, :], 2.0, ALU.mult, ALU.mult))
        c_w8 = s_chain.n
        for ee in range(8):
            w16 = [(s_chain, c_w8)]
            if first_w16[0]:
                w16.append((s_const, CONST_N))
                first_w16[0] = False
            pstep(lambda e, ee=ee: e.tensor_scalar(wex[:, ee * 16:(ee + 1) * 16], ones16_s[:, :],
                                                   w8[:, ee:ee + 1], None, ALU.mult),
                  waits=w16)
        mp = t & 1
        pstep(lambda e, mp=mp, rsb=rsb: e.tensor_mul(midg[mp][:, :], rsb[:, 0:128], wex[:, :]),
              waits=list(midg_free[mp]))
        last_midg[tp_] = s_pool.n
        pend_mg[0] = (t, mp, s_pool.n)

    def emit_mgT(job):
        t, mp, c_mg = job
        tb, tw = mg_ring.acquire()
        P.op("tensor", lambda e, tb=tb, mp=mp: e.matmul(tb[:, 0:128], midg[mp][:, :], identB[:, :],
                                                        start=True, stop=True),
             waits=tw + [(s_pool, c_mg)], inc=s_tp)
        c_t = s_tp.n
        midg_free[mp] = [(s_tp, c_t)]
        P.op("scalar", lambda e, tb=tb, t=t: e.activation(out=midgT[:, t, :], in_=tb[:, 0:128], func=AF.Copy),
             waits=[(s_tp, c_t)], inc=s_mgT)
        mg_ring.release_last((s_mgT, s_mgT.n))
        mgT_counts[t] = s_mgT.n
    ln2t_counts = [0] * NT
    x1b_free = [[], []]
    last_add = [0]
    s1_ap = {}
    s2_info = {}
    def dpost_s1(t):
        tp = t & 1
        def fx(e, t=t):
            return e.dma_start(out=xre[:, :], in_=x_ext[(t + 2) * 128:(t + 3) * 128, :])
        P.op("gpsimd", fx, waits=[(s_pev, pev_counts[t]), (s_chain, last_add[0])],
             inc=s_xre, inc_by=16)
        c_xre = s_xre.n
        c_ap = emit_ln_stats_apply(proj[:, t, :], x1b[tp][:, :],
                                   waits=[(s_pev, pev_counts[t])],
                                   extra_apply_waits=list(x1b_free[tp]))
        s1_ap[t] = (c_ap, c_xre)
    def dpost_s2(t):
        tp = t & 1
        c_ap, c_xre = s1_ap[t]
        P.op("vector", lambda e, tp=tp: e.tensor_add(x1b[tp][:, :], x1b[tp][:, :], xre[:, :]),
             waits=[(s_apply, c_ap), (s_xre, c_xre)], inc=s_chain)
        c_add = s_chain.n
        last_add[0] = c_add
        def fw_(e, t=t, tp=tp):
            return e.dma_start(out=x1_dram[t * 128:(t + 1) * 128, :], in_=x1b[tp][:, :])
        P.op("sync", fw_, waits=[(s_chain, c_add)], inc=s_x1w[tp], inc_by=16)
        c_ap2 = emit_ln_stats_apply(x1b[tp][:, :], proj[:, t, :],
                                    waits=[(s_chain, c_add)])
        x1b_free[tp] = [(s_apply, c_ap2), (s_x1w[tp], s_x1w[tp].n)]
        s2_info[t] = c_ap2
    def dpost_s3(t):
        c_ap2 = s2_info[t]
        def dvf(g, t=t):
            return ln2T[:, 4 * g:4 * g + 4, t * 128:(t + 1) * 128]
        ln2t_counts[t] = emit_transpose_tile(proj[:, t, :], dvf,
                                             src_waits=[(s_apply, c_ap2)])
        emit_router(t)
    for t in range(NT + 2):
        if t >= 2:
            prev_mg = pend_mg[0]
            dpost_s3(t - 2)
            if prev_mg is not None:
                emit_mgT(prev_mg)
        if 1 <= t <= NT:
            dpost_s2(t - 1)
        if t < NT:
            dpost_s1(t)
    emit_mgT(pend_mg[0])
    D_DONE = (s_x1w[0].n, s_x1w[1].n)
    E_DONE = [(s_tpe, s_tpe.n), (s_tpe2, s_tpe2.n)]



    y_ring = Ring([ps[2], ps[3]], init_conds=[(s_pev, D_MM)])
    yev_counts = {}
    for s in range(NSLAB):
        par, c_slab = emit_slab_load(baseT, s, rider=True)
        for t in range(NT):
            bank, yw = y_ring.acquire()
            reg = bank[:, 0:256]
            for d in range(32):
                def f(e, d=d, t=t, reg=reg, par=par):
                    return e.matmul(reg, ln2T[:, d, t * 128:(t + 1) * 128],
                                    slabs[par][:, d, :], start=(d == 0), stop=False)
                P.op("tensor", f,
                     waits=(yw + [(s_slab[par], c_slab), (s_mgT, mgT_counts[t])]) if d == 0 else ())
            def finc(e, t=t, reg=reg, par=par):
                return e.matmul(reg, midgT[:, t, :], brider[par][:, :],
                                start=False, stop=True)
            P.op("tensor", finc, inc=s_ymm)
            c_y = s_ymm.n
            def fe(e, reg=reg, t=t, s=s):
                return e.activation(out=y_nat[:, t, s * OC:(s + 1) * OC], in_=reg, func=AF.Copy)
            P.op("scalar", fe, waits=[(s_ymm, c_y)], inc=s_yev)
            if s == NSLAB - 1:
                yev_counts[t] = s_yev.n
            y_ring.release_last((s_yev, s_yev.n))
        slab_ring.release_last((s_ymm, s_ymm.n))
    F_MM = s_yev.n

    fp_ap = {}
    ost_free = [[], []]
    last_fadd = [0]
    def fpost_s1(t):
        tp = t & 1
        def fx1(e, t=t):
            return e.dma_start(out=x1re[:, :], in_=x1_dram[t * 128:(t + 1) * 128, :])
        P.op("sync", fx1, waits=[(s_yev, yev_counts[t]), (s_chain, last_fadd[0]),
                                 (s_x1w[0], D_DONE[0]), (s_x1w[1], D_DONE[1])], inc=s_x1r, inc_by=16)
        c_r = s_x1r.n
        c_ap = emit_ln_stats_apply(y_nat[:, t, :], outstage[tp][:, :],
                                   waits=[(s_yev, yev_counts[t])],
                                   extra_apply_waits=list(ost_free[tp]))
        fp_ap[t] = (c_ap, c_r)
    def fpost_s2(t):
        tp = t & 1
        c_ap, c_r = fp_ap[t]
        P.op("vector", lambda e, tp=tp: e.tensor_add(outstage[tp][:, :], outstage[tp][:, :], x1re[:, :]),
             waits=[(s_apply, c_ap), (s_x1r, c_r)], inc=s_chain)
        c_add = s_chain.n
        last_fadd[0] = c_add
        def fo(e, t=t, tp=tp):
            return e.dma_start(out=out_hbm[t * 128:(t + 1) * 128, :], in_=outstage[tp][:, :])
        P.op("sync", fo, waits=[(s_chain, c_add)], inc=s_out[tp], inc_by=16)
        ost_free[tp] = [(s_out[tp], s_out[tp].n)]
    for t in range(NT + 1):
        if t >= 1:
            fpost_s2(t - 1)
        if t < NT:
            fpost_s1(t)
    P.op("sync", lambda e: e.nop(), waits=[(s_out[0], s_out[0].n), (s_out[1], s_out[1].n)])

    P.emit()
    es.close()
    return nc


def _host_inputs(inputs):
    x = np.asarray(inputs["L0_attn_x"])
    Wq = np.asarray(inputs["L0_attn_Wq"]); Wk = np.asarray(inputs["L0_attn_Wk"])
    Wv = np.asarray(inputs["L0_attn_Wv"]); Wo = np.asarray(inputs["L0_attn_Wo"])
    baseW = np.asarray(inputs["L0_ffn_base_W"]); routerW = np.asarray(inputs["L0_ffn_router_W"])
    loraA = np.asarray(inputs["L0_ffn_lora_A"]); loraB = np.asarray(inputs["L0_ffn_lora_B"])

    wqT = np.ascontiguousarray(Wq.T)
    wkT = np.ascontiguousarray(Wk.T)
    wvT = np.ascontiguousarray(Wv.T)
    woT = np.ascontiguousarray(Wo.T)
    baseT = np.ascontiguousarray(baseW.T)
    routerT = np.ascontiguousarray(routerW.T)
    a_all = np.ascontiguousarray(loraA.transpose(1, 0, 2).reshape(D, 128))
    b_all = np.ascontiguousarray(loraB.reshape(128, D))
    cos, sin = _rope_cache()
    sinp = np.concatenate([-sin[:, :64], sin[:, 64:]], axis=1)
    identity = np.eye(128, dtype=np.float32)
    onescol = np.ones((128, 1), dtype=np.float32)
    ones16 = np.ones((128, 16), dtype=np.float32)
    lin = np.linspace(0.0, 1.0, 256, dtype=np.float32)

    in_maps = []
    for core in range(NCORES):
        b, q = core // 4, core % 4
        t0 = 1024 * q
        x_ext_a = np.zeros((TEXT, D), dtype=np.float32)
        lo = max(0, t0 - 256); hi = min(4096, t0 + 1280)
        x_ext_a[lo - (t0 - 256): hi - (t0 - 256)] = x[b, lo:hi]
        cos_own = cos[t0:t0 + TOWN]; sinp_own = sinp[t0:t0 + TOWN]
        cos2_a = np.ascontiguousarray(np.concatenate([cos_own, cos_own], axis=1))
        sinp2_a = np.ascontiguousarray(np.concatenate([sinp_own, sinp_own], axis=1))
        cosk_a = np.zeros((TEXT, DH), dtype=np.float32)
        sinkp_a = np.zeros((TEXT, DH), dtype=np.float32)
        cosk_a[lo - (t0 - 256): hi - (t0 - 256)] = cos[lo:hi]
        sinkp_a[lo - (t0 - 256): hi - (t0 - 256)] = sinp[lo:hi]
        kbias_a = np.zeros((128, 24), dtype=np.float32)
        if q == 3:
            kbias_a[:, 3 * 6 + 4] = -30000.0
            kbias_a[:, 3 * 6 + 5] = -30000.0
        alpha_a = np.zeros((128, 64), dtype=np.float32)
        for c in range(4):
            a = lin.copy()
            if q == 0 and c == 0:
                a = np.ones(256, dtype=np.float32)
            for h4 in range(4):
                for qh in range(2):
                    alpha_a[:, c * 16 + h4 * 4 + 0 * 2 + qh] = 1.0 - a[qh * 128:(qh + 1) * 128]
                    alpha_a[:, c * 16 + h4 * 4 + 1 * 2 + qh] = a[qh * 128:(qh + 1) * 128]
        in_maps.append({
            "x_ext": x_ext_a, "wqT": wqT, "wkT": wkT, "wvT": wvT, "woT": woT,
            "baseT": baseT, "routerT": routerT, "a_all": a_all, "b_all": b_all,
            "cos2": cos2_a, "sinp2": sinp2_a, "cosk": cosk_a, "sinkp": sinkp_a,
            "kbias": kbias_a, "alpha_exp": alpha_a, "identity": identity,
            "onescol": onescol, "ones16": ones16,
        })
    return in_maps


def kernel(**inputs):
    in_maps = _host_inputs(inputs)
    nc = build_program()
    res = run_bass_kernel_spmd(nc, in_maps, core_ids=list(range(NCORES)))
    out = np.empty((2, 4096, D), dtype=np.float32)
    for core in range(NCORES):
        b, q = core // 4, core % 4
        out[b, 1024 * q:1024 * (q + 1)] = res.results[core]["out"]
    return out
